# version 12
# baseline (speedup 1.0000x reference)
import numpy as np
import ml_dtypes
import concourse.bass as bass
import concourse.mybir as mybir
from concourse.bass_utils import run_bass_kernel_spmd

I16 = mybir.dt.int16
F16 = mybir.dt.float16
F32 = mybir.dt.float32
F32R = mybir.dt.float32r
BF16 = mybir.dt.bfloat16
I32 = mybir.dt.int32
AF = mybir.ActivationFunctionType
ALU = mybir.AluOpType
AX = mybir.AxisListType
BF = ml_dtypes.bfloat16

D = 1024
H = 16
DH = 64
DH4 = 4096
T = 128
NSEQ = 2
SEQLEN = 2048
NT = SEQLEN // T


def _host_consts():
    c = {}
    c["idn16"] = np.eye(128, dtype=np.float32).astype(BF)
    tri = np.zeros((128, 128), np.float32)
    for blk in range(4):
        for j in range(32):
            tri[blk * 32 + j, blk * 32 + j:blk * 32 + 32] = -1.0
    c["tri32"] = tri
    m = np.zeros((128, 512), np.float32)
    for r in range(128):
        k = r % 64
        for hh in range(8):
            for q in range(64):
                if k // 32 == q // 32 and k % 32 <= q % 32:
                    m[r, hh * 64 + q] = 1.0
    c["maskp"] = m.astype(BF)
    rs = np.zeros((128, 8, 128), np.float32)
    for cc in range(4):
        rs[32 * cc + 31, 2 * cc, 0:64] = 1.0
        rs[32 * cc + 31, 2 * cc + 1, 64:128] = 1.0
    c["rowsel16"] = rs.astype(BF)
    c["ones_col16"] = np.ones((128, 1), np.float32).astype(BF)
    c["ones_row16"] = np.ones((1, 128), np.float32).astype(BF)
    c["ones_row32"] = np.ones((1, 128), np.float32)
    c["ones_col32"] = np.ones((128, 1), np.float32)
    shm = np.zeros((128, 128), np.float32)
    for t in range(1, 128):
        shm[t - 1, t] = 1.0
    shp = np.zeros((128, 128), np.float32)
    for t in range(127):
        shp[t + 1, t] = 1.0
    e31 = np.zeros((1, 32), np.float32); e31[0, 31] = 1.0
    sh1 = np.zeros((128, 128), np.float32)
    for t in range(1, 128):
        sh1[t - 1, t] = 1.0
    sh2 = np.zeros((128, 128), np.float32)
    for t in range(2, 128):
        sh2[t - 2, t] = 1.0
    c["sh1_r"] = sh1
    c["sh2_r"] = sh2
    c["idn_r"] = np.eye(128, dtype=np.float32)
    er = np.zeros((4, 128), np.float32)
    er[0, 0] = 1.0
    er[1, 0] = 1.0
    er[2, 1] = 1.0
    er[3, :] = 1.0
    c["erows_h"] = er
    return c


_CONST = _host_consts()


class Op:
    __slots__ = ("block", "eng", "fn", "waits", "mark", "dmas", "dsem", "drains")
    def __init__(self, block, eng, fn, waits=(), mark=None, dmas=0, dsem=None,
                 drains=()):
        self.block, self.eng, self.fn = block, eng, fn
        self.waits, self.mark, self.dmas = [w for w in waits if w], mark, dmas
        self.dsem = dsem
        self.drains = [d for d in drains if d]


class SpacedEng:
    _ALU = {"tensor_tensor", "tensor_scalar", "tensor_copy",
            "scalar_tensor_tensor", "tensor_reduce", "reduce_max", "memset",
            "reciprocal", "activation", "copy", "iota"}

    def __init__(self, eng, junk):
        self._eng, self._junk = eng, junk
        self._narrow = False

    @staticmethod
    def _width(a):
        try:
            return int(np.prod([d[1] for d in a.ap[1:]]))
        except Exception:
            return 1 << 30

    def __getattr__(self, nm):
        f = getattr(self._eng, nm)
        if nm not in SpacedEng._ALU or not callable(f):
            return f

        def wrap(*args, **kw):
            if self._narrow:
                self._eng.tensor_scalar(self._junk[:, :], self._junk[:, :],
                                        1.0, None, ALU.mult)
            r = f(*args, **kw)
            out = args[0] if args else None
            self._narrow = (out is not None and hasattr(out, "ap")
                            and SpacedEng._width(out) < 64)
            return r
        return wrap


class Sched:
    def __init__(self, start=None):
        self.ops = []
        self.marks = {}
        self.counts = dict(start) if start else {k: 0 for k in ("Y", "T", "V", "S", "G", "GD")}
    def add(self, block, eng, fn, waits=(), mark=None, dmas=0, dsem=None,
            drains=()):
        assert (dmas == 0) == (dsem is None), f"dma op needs dsem: {mark}"
        self.ops.append(Op(block, eng, fn, waits, mark, dmas, dsem, drains))
    def plan(self):
        for op in self.ops:
            if op.dmas:
                key = op.dsem
                self.counts[key] = self.counts.get(key, 0) + 16 * op.dmas
                if op.mark:
                    self.marks[op.mark] = (key, self.counts[key])
            elif op.mark:
                self.counts[op.eng] += 1
                self.marks[op.mark] = (op.eng, self.counts[op.eng])
        missing = [w for op in self.ops for w in op.waits if w not in self.marks]
        assert not missing, f"unresolved waits: {sorted(set(missing))[:10]}"
    def emit_stream(self, eng_key, eng, sems, block_idx, junk=None):
        last_wait = {}
        spaced = SpacedEng(eng, junk) if junk is not None and \
            eng_key in ("V", "G") else None
        for op in self.ops:
            if op.block != block_idx or op.eng != eng_key:
                continue
            for w in op.waits:
                sk, val = self.marks[w]
                if sk == eng_key:
                    continue
                if last_wait.get(sk, -1) >= val:
                    continue
                eng.wait_ge(sems[sk], val)
                last_wait[sk] = val
            for d in op.drains:
                sk, val = self.marks[d]
                eng.wait_ge(sems[sk], val)
            if op.dmas:
                def dma(out, in_, _k=op.dsem, _e=eng, **kw):
                    return _e.dma_start(out, in_, **kw).then_inc(sems[_k], 16)
                op.fn(eng, dma)
            else:
                inst = op.fn(spaced if spaced is not None else eng)
                if op.mark:
                    assert inst is not None, f"mark {op.mark} with no instruction"
                    inst.then_inc(sems[op.eng], 1)


def build_program(n_tiles=NT, debug=False, max_block=4):
    nc = bass.Bass("TRN2", target_bir_lowering=False, detect_race_conditions=False)
    S = Sched()

    d_x = nc.dram_tensor("x", [NSEQ, SEQLEN, D], F32, kind="ExternalInput")
    d_w = {n: nc.dram_tensor(n, [D, D], F32, kind="ExternalInput")
           for n in ("wq", "wk", "wv", "wg", "wo")}
    d_bg = nc.dram_tensor("bg", [1, D], F32, kind="ExternalInput")
    d_w1 = nc.dram_tensor("w1", [DH4, D], F32, kind="ExternalInput")
    d_w2 = nc.dram_tensor("w2", [D, DH4], F32, kind="ExternalInput")
    d_cwT = nc.dram_tensor("cwT", [3, DH4], F32, kind="ExternalInput")
    d_cb = nc.dram_tensor("cb", [1, DH4], F32, kind="ExternalInput")
    dc = {}
    for name, arr in _CONST.items():
        dt = BF16 if arr.dtype == BF else (F16 if arr.dtype == np.float16 else F32)
        dc[name] = nc.dram_tensor(name, list(arr.shape), dt, kind="ExternalInput")
    d_out = nc.dram_tensor("out", [NSEQ, SEQLEN, D], F32, kind="ExternalOutput")
    _ik = "ExternalOutput" if debug else "Internal"
    d_s_scr = nc.dram_tensor("s_scr", [NSEQ, SEQLEN, D], BF16, kind=_ik)
    d_xq_scr = nc.dram_tensor("xq_scr", [NSEQ, NT, 128, 8, 128], BF16, kind=_ik)
    d_s1_scr = nc.dram_tensor("s1_scr", [NSEQ, NT, 128, 1], F32, kind=_ik)
    d_w1q_scr = nc.dram_tensor("w1q_scr", [DH4, D], BF16, kind=_ik)
    d_w2q_scr = nc.dram_tensor("w2q_scr", [D, DH4], BF16, kind=_ik)
    d_y2a_scr = nc.dram_tensor("y2a_scr", [NSEQ, NT + 1, 128, 32, 128], BF16, kind=_ik)
    d_s2_scr = nc.dram_tensor("s2_scr", [NSEQ, NT + 1, 128, 1], F32, kind=_ik)
    dbg = {}
    if debug:
        for nm, shp, dt in [("dbg_qk", [128, 2048], BF16), ("dbg_sp", [128, 1024], F32),
                            ("dbg_vt", [128, 1024], BF16), ("dbg_a16", [128, 1024], BF16),
                            ("dbg_s0", [128, 512], BF16), ("dbg_sout", [128, 1024], BF16),
                            ("dbg_xq", [128, 1024], BF16), ("dbg_h1", [128, 4096], BF16),
                            ("dbg_ys", [128, 4096], F32), ("dbg_y2s", [128, 4096], BF16),
                            ("dbg_g", [128, 1024], F32),
                            ("dbg_xb", [128, 1024], F32),
                            ("dbg_x16b", [128, 1024], BF16),
                            ("dbg_cols", [128, 24], F32),
                            ("dbg_x16a", [128, 8, 128], BF16)]:
            dbg[nm] = nc.dram_tensor(nm, shp, dt, kind="ExternalOutput")

    import contextlib
    with contextlib.ExitStack() as ctx:
        alloc = lambda *a: ctx.enter_context(nc.sbuf_tensor(*a))
        palloc = lambda *a: ctx.enter_context(nc.psum_tensor(*a))
        ctxA = contextlib.ExitStack()
        allocA = lambda *a: ctxA.enter_context(nc.sbuf_tensor(*a))
        pallocA = lambda *a: ctxA.enter_context(nc.psum_tensor(*a))
        _A_ONLY = {"tri32", "maskp", "rowsel16"}
        _B_ONLY = {"sh1_r", "sh2_r", "idn_r"}
        cst = {}
        def _dt_of(arr):
            if arr.dtype == BF:
                return BF16
            if arr.dtype == np.float16:
                return F16
            return F32
        for name, arr in _CONST.items():
            if name in _A_ONLY or name in _B_ONLY:
                continue
            cst[name] = alloc(name + "_sb", list(arr.shape), _dt_of(arr))
        swcols = alloc("swcols", [128, 4], F32)
        junkc = alloc("junkc", [128, 64], F32)
        smalls = alloc("smalls", [1, 8], F32)
        class SemPool(dict):
            def __missing__(self, k):
                self[k] = ctx.enter_context(nc.semaphore("sem" + k))
                return self[k]
        sems = SemPool()
        for k in ("T", "V", "S", "G"):
            sems[k]
        tdc = [0]
        def tdrain(eng, last):
            tdc[0] += 1
            last.then_inc(sems["TD"], 1)
            eng.wait_ge(sems["TD"], tdc[0])
        for name in _A_ONLY:
            cst[name] = allocA(name + "_sb", list(_CONST[name].shape), _dt_of(_CONST[name]))
        bgB = allocA("bgB", [128, D], BF16)
        W4 = allocA("W4", [128, 8, 4, 1024], BF16)
        WO = allocA("WO", [128, 8, 1024], BF16)
        corrL = allocA("corrL", [1, 128], BF16)
        corrR = allocA("corrR", [1, 4096], BF16)
        bgrow = allocA("bgrow", [1, D], BF16)
        xB = [allocA(f"xB{i}", [128, D], F32) for i in range(2)]
        colsD = [allocA(f"cols{i}", [128, 24], F32) for i in range(2)]
        mnegD = [allocA(f"mneg{i}", [128, 1], BF16) for i in range(2)]
        x16B = allocA("x16B", [128, D], BF16)
        x16A = allocA("x16A", [128, 8, 128], BF16)
        QKB = allocA("QKB", [128, 2048], BF16)
        QA = allocA("QA", [128, 8, 128], BF16)
        KA = allocA("KA", [128, 8, 128], BF16)
        VT16 = allocA("VT16", [128, D], BF16)
        VT16z = allocA("VT16z", [128, D], BF16)
        QAzO = allocA("QAzO", [128, 8, 128], BF16)
        A16 = allocA("A16", [128, D], BF16)
        AREC = allocA("AREC", [128, D], BF16)
        SCRZ = allocA("SCRZ", [128, D], F32)
        SCRE = allocA("SCRE", [128, D], F32)
        PTM = allocA("PTM", [128, 1024], BF16)
        SST = [[allocA(f"SST{s}_{p}", [128, 512], BF16) for p in range(2)] for s in range(NSEQ)]
        SCR_S = allocA("SCR_S", [128, 512], F32)
        OB16 = allocA("OB16", [128, D], BF16)
        OA16 = allocA("OA16", [128, 8, 128], BF16)
        sB16 = allocA("sB16", [128, D], BF16)
        T0 = allocA("T0", [128, D], F32)
        ICST = allocA("ICST", [128, D], I32)
        XQ1B = allocA("XQ1B", [128, D], BF16)
        XQ1A = allocA("XQ1A", [128, 8, 128], BF16)
        WCH = [allocA(f"WCH{i}", [128, 1024], F32) for i in range(2)]
        WQ16 = allocA("WQ16", [128, 1024], BF16)
        WT0 = WCH[1]
        WIC = allocA("WIC", [128, 1024], I32)
        abscols = allocA("abscols", [128, 80], F32)
        pP = pallocA("pP", [128, 2048], F32)
        pPT = pallocA("pPT", [128, 512], F32)
        pU = pallocA("pU", [128, 512], F32)
        pBAC = pallocA("pBAC", [128, 512], F32)
        pTR = pallocA("pTR", [128, 1024], BF16)

        B0 = 0
        def ld_consts(eng, dma):
            for name in _CONST:
                if name in _B_ONLY:
                    continue
                idx = tuple(slice(None) for _ in _CONST[name].shape)
                dma(cst[name][idx], dc[name][idx])
        S.add(B0, "Y", ld_consts, dmas=len(_CONST) - 3, mark="c_ld", dsem="dcst")

        def ld_weights(eng, dma):
            for wi, n in enumerate(("wq", "wk", "wv", "wg")):
                dma(W4[:, :, wi, :], d_w[n][:, :].rearrange("(k p) n -> p k n", p=128))
            dma(WO[:, :, :], d_w["wo"][:, :].rearrange("(k p) n -> p k n", p=128))
            dma(bgrow[:, :], d_bg[:, :])
        S.add(B0, "G", ld_weights, dmas=6, mark="w_ld", dsem="dwt")

        def init_s(eng):
            eng.memset(junkc[:, :], 0.0)
            for s in range(NSEQ):
                for p in range(2):
                    eng.memset(SST[s][p][:, :], 0.0)
            eng.memset(abscols[:, :], 0.0)
            eng.memset(smalls[:, :], 0.0)
            eng.memset(VT16z[:, :], 0.0)
            eng.memset(QAzO[:, :, :], 0.0)
            eng.memset(colsD[0][:, :], 0.0)
            return eng.memset(colsD[1][:, :], 0.0)
        S.add(B0, "V", init_s, mark="v_sinit")

        def bcasts(eng, dma):
            ap = bass.AP(d_bg[0:1, :].tensor, d_bg[0:1, :].offset,
                         [[0, 128]] + [list(p) for p in d_bg[0:1, :].ap[1:]])
            dma(bgB[:, :], ap)
        S.add(B0, "G", bcasts, waits=["c_ld"], dmas=1, mark="v_bgb", dsem="dbc")
        S.add(B0, "V", lambda eng: eng.tensor_copy(smalls[0:1, 7:8], smalls[0:1, 7:8]),
              waits=["v_bgb"], mark="v_cwb_2_7")
        for rnd in range(2):
            def mm_wsums(eng, rnd=rnd):
                last = None
                for nb in range(4):
                    nbg = rnd * 4 + nb
                    for k in range(8):
                        last = eng.matmul(pP[0:1, nb * 512:(nb + 1) * 512],
                                          cst["ones_col16"][:, :],
                                          W4[:, k, nbg // 2, (nbg % 2) * 512:(nbg % 2) * 512 + 512],
                                          start=(k == 0), stop=(k == 7))
                return last
            S.add(B0, "T", mm_wsums, waits=["w_ld", "c_ld", "v_cwb_2_7", "v_bgb"] + (["v_corrR0"] if rnd else []),
                  mark=f"t_wsums{rnd}")
            S.add(B0, "V", lambda eng, rnd=rnd: eng.tensor_copy(
                corrR[0:1, rnd * 2048:(rnd + 1) * 2048], pP[0:1, 0:2048]),
                waits=[f"t_wsums{rnd}"], mark=f"v_corrR{rnd}")
        S.add(B0, "V", lambda eng: eng.tensor_copy(smalls[0:1, 6:7], smalls[0:1, 6:7]),
              waits=["v_corrR1"], mark="v_corrR")


        B1 = 1
        NITER = 2 * n_tiles
        wpieces = [("w1", i, 0) for i in range(32)] + \
                  [("w2", i, q) for i in range(8) for q in range(4)]
        def wp_src(p):
            wname, ci, q = p
            if wname == "w1":
                return d_w1[ci * 128:(ci + 1) * 128, :], d_w1q_scr[ci * 128:(ci + 1) * 128, :]
            return (d_w2[ci * 128:(ci + 1) * 128, q * 1024:(q + 1) * 1024],
                    d_w2q_scr[ci * 128:(ci + 1) * 128, q * 1024:(q + 1) * 1024])

        n_stat_iters = min(8, NITER)
        stats_per_iter = -(-len(wpieces) // n_stat_iters)
        fin_iter = n_stat_iters - 1 if NITER <= n_stat_iters + 1 else n_stat_iters
        q_start = fin_iter + 1 if fin_iter + 1 < NITER else fin_iter
        n_q_iters = max(1, NITER - q_start)
        quant_per_iter = -(-len(wpieces) // n_q_iters)
        last_q_mark = [None]

        for it in range(NITER):
            t, s = it // 2, it % 2
            ib = it % 2
            P = f"i{it}_"
            Pp = f"i{it - 1}_"
            Ppp = f"i{it - 2}_"

            def w_(name, lag=0):
                if it - lag < 0:
                    return None
                return f"i{it - lag}_{name}"


            def emit_stats(it2):
                ib2 = it2 % 2
                P2 = f"i{it2}_"
                cols = colsD[ib2]
                mneg16 = mnegD[ib2]
                t2, s2 = it2 // 2, it2 % 2
                def w2_(name, lag=0):
                    if it2 - lag < 0:
                        return None
                    return f"i{it2 - lag}_{name}"
                def op_ldx(eng, dma, t2=t2, s2=s2, ib2=ib2):
                    dma(xB[ib2][:, :], d_x[s2, t2 * T:(t2 + 1) * T, :])
                S.add(B1, "Y", op_ldx,
                      waits=[w for w in (w2_("v_q1", 2), w2_("ssumsq", 2)) if w],
                      dmas=1, mark=P2 + "dx", dsem=f"dx{ib2}")
                S.add(B1, "S", lambda eng: eng.activation(
                    SCRE[:, :], xB[ib2][:, :], AF.Identity, bias=0.0, scale=1.0,
                    accum_out=cols[:, 0:1]),
                    waits=[w for w in (P2 + "dx", w2_("v_q1", 2), w2_("g_xq", 2))
                           if w], mark=P2 + "ssum")
                S.add(B1, "S", lambda eng: eng.activation(
                    SCRE[:, :], xB[ib2][:, :], AF.Square, bias=0.0, scale=1.0,
                    accum_out=cols[:, 1:2]), waits=[P2 + "dx"], mark=P2 + "ssumsq")
                def op_vmx(eng):
                    eng.reduce_max(cols[:, 5:6], xB[ib2][:, :], AX.X)
                    return eng.tensor_reduce(cols[:, 6:7], xB[ib2][:, :], AX.X,
                                             ALU.min)
                S.add(B1, "V", op_vmx, waits=[P2 + "dx"], mark=P2 + "vmx")
                def op_stats(eng):
                    eng.tensor_scalar(cols[:, 2:3], cols[:, 0:1], 1.0 / D, None, ALU.mult)
                    eng.tensor_tensor(cols[:, 13:14], cols[:, 2:3], cols[:, 2:3], ALU.mult)
                    eng.tensor_scalar(cols[:, 14:15], cols[:, 1:2], 1.0 / D, None, ALU.mult)
                    eng.tensor_tensor(cols[:, 3:4], cols[:, 14:15], cols[:, 13:14],
                                      ALU.subtract)
                    eng.tensor_scalar(cols[:, 3:4], cols[:, 3:4], 1e-5, None, ALU.add)
                    return eng.tensor_scalar(mneg16[:, :], cols[:, 2:3], -1.0, None,
                                             ALU.mult)
                S.add(B1, "V", op_stats,
                      waits=[w for w in (P2 + "ssum", P2 + "ssumsq",
                                         w2_("t_mrow", 2)) if w],
                      mark=P2 + "vstat")
                S.add(B1, "S", lambda eng: eng.activation(
                    cols[:, 15:16], cols[:, 3:4], AF.Ln, bias=0.0, scale=1.0),
                    waits=[P2 + "vstat"], mark=P2 + "lnv")
                S.add(B1, "V", lambda eng: eng.tensor_scalar(
                    cols[:, 16:17], cols[:, 15:16], -0.5, None, ALU.mult),
                    waits=[P2 + "lnv"], mark=P2 + "nhalf")
                S.add(B1, "S", lambda eng: eng.activation(
                    cols[:, 4:5], cols[:, 16:17], AF.Exp, bias=0.0, scale=1.0),
                    waits=[P2 + "nhalf"], mark=P2 + "r")
                def op_rfix(eng):
                    eng.tensor_tensor(cols[:, 22:23], cols[:, 4:5], cols[:, 4:5], ALU.mult)
                    eng.tensor_tensor(cols[:, 22:23], cols[:, 22:23], cols[:, 3:4], ALU.mult)
                    eng.tensor_scalar(cols[:, 22:23], cols[:, 22:23], -0.5, 1.5,
                                      ALU.mult, ALU.add)
                    return eng.tensor_tensor(cols[:, 4:5], cols[:, 4:5], cols[:, 22:23],
                                             ALU.mult)
                S.add(B1, "V", op_rfix, waits=[P2 + "r"], mark=P2 + "rfix")
                def op_qcols(eng):
                    eng.tensor_tensor(cols[:, 17:18], cols[:, 5:6], cols[:, 2:3], ALU.subtract)
                    eng.tensor_tensor(cols[:, 18:19], cols[:, 2:3], cols[:, 6:7], ALU.subtract)
                    eng.tensor_tensor(cols[:, 18:19], cols[:, 17:18], cols[:, 18:19], ALU.max)
                    eng.tensor_tensor(cols[:, 19:20], cols[:, 18:19], cols[:, 4:5], ALU.mult)
                    eng.tensor_scalar(cols[:, 7:8], cols[:, 19:20], 1e-5, None, ALU.max)
                    eng.reciprocal(cols[:, 20:21], cols[:, 7:8])
                    eng.tensor_tensor(cols[:, 23:24], cols[:, 7:8], cols[:, 20:21], ALU.mult)
                    eng.tensor_scalar(cols[:, 23:24], cols[:, 23:24], -1.0, 2.0,
                                      ALU.mult, ALU.add)
                    eng.tensor_tensor(cols[:, 20:21], cols[:, 20:21], cols[:, 23:24], ALU.mult)
                    eng.tensor_scalar(cols[:, 8:9], cols[:, 20:21], 127.0, None, ALU.mult)
                    eng.tensor_scalar(cols[:, 9:10], cols[:, 7:8], 1.0 / 127.0, None, ALU.mult)
                    eng.tensor_tensor(cols[:, 10:11], cols[:, 4:5], cols[:, 8:9], ALU.mult)
                    eng.tensor_tensor(cols[:, 21:22], cols[:, 2:3], cols[:, 10:11], ALU.mult)
                    return eng.tensor_scalar(cols[:, 11:12], cols[:, 21:22], -1.0, None,
                                             ALU.mult)
                S.add(B1, "V", op_qcols,
                      waits=[w for w in (P2 + "rfix", P2 + "vmx",
                                         w2_("g_xq", 2)) if w],
                      mark=P2 + "qcols")

            cols = colsD[ib]
            mneg16 = mnegD[ib]
            if it == 0:
                emit_stats(0)
            S.add(B1, "V", lambda eng, ib=ib: eng.tensor_copy(
                x16B[:, :], xB[ib][:, :]),
                waits=[P + "dx", w_("t_x16t", 1)], mark=P + "vmm")

            def op_trm(eng, mneg16=mneg16):
                return eng.transpose(pTR[0:1, 0:128], mneg16[:, 0:1], cst["idn16"][:, :])
            S.add(B1, "T", op_trm, waits=[P + "vstat", w_("s_xqa", 1) or "v_sinit"],
                  mark=P + "t_mrow")
            S.add(B1, "S", lambda eng: eng.copy(corrL[0:1, 0:128], pTR[0:1, 0:128]),
                  waits=[P + "t_mrow", w_("t_vg", 1)], mark=P + "corrl")

            def op_x16t(eng):
                last = None
                for k in range(8):
                    last = eng.transpose(pTR[:, k * 128:(k + 1) * 128],
                                         x16B[:, k * 128:(k + 1) * 128], cst["idn16"][:, :])
                return last
            S.add(B1, "T", op_x16t, waits=[P + "vmm", P + "corrl"], mark=P + "t_x16t")
            S.add(B1, "S", lambda eng: eng.copy(x16A[:, :, :], pTR[:, 0:1024]),
                  waits=[P + "t_x16t"], mark=P + "s_x16a")
            if debug and it == 0:
                S.add(B1, "G", lambda eng, dma: dma(
                    dbg["dbg_x16a"][:, :, :], x16A[:, :, :]),
                    waits=[P + "s_x16a"], dmas=1, dsem="ddbg")

            def op_qk(eng):
                last = None
                for k in range(8):
                    for nb in range(4):
                        last = eng.matmul(pP[:, nb * 512:(nb + 1) * 512],
                                          x16A[:, k, :],
                                          W4[:, k, nb // 2, (nb % 2) * 512:(nb % 2) * 512 + 512],
                                          start=(k == 0), stop=False)
                for nb in range(4):
                    last = eng.matmul(pP[:, nb * 512:(nb + 1) * 512], corrL[0:1, :],
                                      corrR[0:1, nb * 512:(nb + 1) * 512],
                                      start=False, stop=True)
                return last
            S.add(B1, "T", op_qk,
                  waits=[P + "s_x16a", "v_corrR", w_("v_sevac", 1), w_("v_oevac", 1)],
                  mark=P + "t_qk")

            S.add(B1, "S", lambda eng, cols=cols: eng.activation(
                QKB[:, :], pP[:, :], AF.Identity, bias=0.0, scale=cols[:, 4:5]),
                waits=[P + "t_qk", P + "rfix", w_("t_u3", 1)], mark=P + "v_qkev")
            if debug and it == 0:
                S.add(B1, "G", lambda eng, dma: dma(dbg["dbg_qk"][:, :], QKB[:, :]),
                      waits=[P + "v_qkev"], dmas=1, dsem="ddbg")

            def op_vg(eng):
                last = None
                for k in range(8):
                    for nb in range(4):
                        last = eng.matmul(pP[:, nb * 512:(nb + 1) * 512],
                                          x16A[:, k, :],
                                          W4[:, k, 2 + nb // 2, (nb % 2) * 512:(nb % 2) * 512 + 512],
                                          start=(k == 0), stop=False)
                for nb in range(4):
                    last = eng.matmul(pP[:, nb * 512:(nb + 1) * 512], corrL[0:1, :],
                                      corrR[0:1, 2048 + nb * 512:2048 + (nb + 1) * 512],
                                      start=False, stop=True)
                return last
            S.add(B1, "T", op_vg, waits=[P + "v_qkev"], mark=P + "t_vg")

            def op_qt(eng):
                last = None
                for k in range(8):
                    last = eng.transpose(pTR[:, k * 128:(k + 1) * 128],
                                         QKB[:, k * 128:(k + 1) * 128], cst["idn16"][:, :])
                return last
            S.add(B1, "T", op_qt, waits=[P + "v_qkev", P + "s_x16a"], mark=P + "t_qt")
            S.add(B1, "S", lambda eng: eng.copy(QA[:, :, :], pTR[:, 0:1024]),
                  waits=[P + "t_qt", w_("t_oeA1", 1), w_("v_qaz", 1)],
                  mark=P + "s_qa")
            def op_kt(eng):
                last = None
                for k in range(8):
                    last = eng.transpose(pTR[:, k * 128:(k + 1) * 128],
                                         QKB[:, 1024 + k * 128:1024 + (k + 1) * 128],
                                         cst["idn16"][:, :])
                return last
            S.add(B1, "T", op_kt, waits=[P + "s_qa"], mark=P + "t_kt")
            S.add(B1, "S", lambda eng: eng.copy(KA[:, :, :], pTR[:, 0:1024]),
                  waits=[P + "t_kt", w_("t_p11", 1)], mark=P + "s_ka")

            S.add(B1, "V", lambda eng, cols=cols: eng.scalar_tensor_tensor(
                SCRZ[:, :], pP[:, 1024:2048], cols[:, 4:5], bgB[:, :], ALU.mult, ALU.add),
                waits=[P + "t_vg", "v_bgb", w_("t_cums", 1)], mark=P + "v_zev")
            S.add(B1, "S", lambda eng: eng.activation(
                SCRE[:, :], SCRZ[:, :], AF.Exp, bias=0.0, scale=-1.0),
                waits=[P + "v_zev"], mark=P + "s_spe")
            S.add(B1, "S", lambda eng: eng.activation(
                SCRZ[:, :], SCRE[:, :], AF.Ln, bias=1.0, scale=1.0),
                waits=[P + "s_spe"], mark=P + "s_sp")
            if debug and it == 0:
                S.add(B1, "G", lambda eng, dma: dma(dbg["dbg_sp"][:, :], SCRZ[:, :]),
                      waits=[P + "s_sp"], dmas=1, dsem="ddbg")

            for pi_ in range(2):
                pr_ = 64 * pi_
                for g in range(2):
                    def op_pg(eng, pr=pr_, g=g):
                        last = None
                        for hh in (0, 2, 4, 6, None, 1, 3, 5, 7):
                            if hh is None:
                                tdrain(eng, last)
                                continue
                            h = g * 8 + hh
                            pb = 64 * (h % 2)
                            last = eng.matmul(pPT[pr:pr + 64, hh * 64:(hh + 1) * 64],
                                              KA[pb:pb + 64, h // 2, pr:pr + 64],
                                              QA[pb:pb + 64, h // 2, pr:pr + 64],
                                              start=True, stop=True)
                        return last
                    wp = [P + "s_ka"]
                    if g:
                        wp.append(P + f"v_ptm{pi_}0")
                    else:
                        wp.append((P + f"v_ptm{pi_ - 1}1") if pi_
                                  else w_("v_ptm11", 1))
                    if it == q_start and pi_ == 0 and g == 0 and q_start > fin_iter:
                        wp.append("v_swcols")
                    S.add(B1, "T", op_pg, waits=[w for w in wp if w],
                          mark=P + f"t_p{pi_}{g}")
                    S.add(B1, "V", lambda eng, pr=pr_, g=g: eng.tensor_tensor(
                        PTM[pr:pr + 64, g * 512:(g + 1) * 512],
                        pPT[pr:pr + 64, 0:512],
                        cst["maskp"][pr:pr + 64, :], ALU.mult),
                        waits=[P + f"t_p{pi_}{g}"] +
                              ([w_(f"t_oi{pi_}", 1)] if w_(f"t_oi{pi_}", 1) else []),
                        mark=P + f"v_ptm{pi_}{g}")

            def op_cums(eng):
                eng.matmul(pP[:, 1024:1536], cst["tri32"][:, :], SCRZ[:, 0:512],
                           start=True, stop=True)
                return eng.matmul(pP[:, 1536:2048], cst["tri32"][:, :], SCRZ[:, 512:1024],
                                  start=True, stop=True)
            S.add(B1, "T", op_cums, waits=[P + "s_sp", P + "v_zev"], mark=P + "t_cums")

            S.add(B1, "S", lambda eng: eng.activation(
                A16[:, :], pP[:, 1024:2048], AF.Exp, bias=0.0, scale=1.0),
                waits=[P + "t_cums", w_("v_oevac", 1), w_("t_bac3", 1)], mark=P + "s_a16")
            S.add(B1, "S", lambda eng: eng.activation(
                AREC[:, :], pP[:, 1024:2048], AF.Exp, bias=0.0, scale=-1.0),
                waits=[P + "t_cums", w_("v_vt", 1)], mark=P + "s_arec")
            if debug and it == 0:
                S.add(B1, "G", lambda eng, dma: dma(dbg["dbg_a16"][:, :], A16[:, :]),
                      waits=[P + "s_a16"], dmas=1, dsem="ddbg")

            S.add(B1, "V", lambda eng, cols=cols: eng.scalar_tensor_tensor(
                VT16[:, :], pP[:, 0:1024], cols[:, 4:5], AREC[:, :], ALU.mult, ALU.mult),
                waits=[P + "s_arec", P + "t_vg", w_("t_u3", 1)], mark=P + "v_vt")
            if debug and it == 0:
                S.add(B1, "G", lambda eng, dma: dma(dbg["dbg_vt"][:, :], VT16[:, :]),
                      waits=[P + "v_vt"], dmas=1, dsem="ddbg")

            S.add(B1, "V", lambda eng: (
                eng.tensor_copy(QAzO[:, :, 32:64], QA[:, :, 32:64]),
                eng.tensor_copy(QAzO[:, :, 96:128], QA[:, :, 96:128]))[-1],
                waits=[P + "s_qa", w_("t_oeB1", 1)], mark=P + "v_qaz")
            S.add(B1, "V", lambda eng: eng.tensor_copy(
                VT16z[96:128, :], VT16[96:128, :]),
                waits=[P + "v_vt", w_("t_u3", 1)], mark=P + "v_vtz")

            if it + 1 < NITER:
                emit_stats(it + 1)

            sel = SST[s]

            def add_bac(cc):
                def op_bac(eng, cc=cc):
                    av = A16[:, :].rearrange("p (a b) -> p a b", a=8)
                    eng.matmul(pBAC[:, :], cst["rowsel16"][:, 2 * cc, :], av[:, :, 0:64],
                               start=True, stop=False)
                    return eng.matmul(pBAC[:, :], cst["rowsel16"][:, 2 * cc + 1, :],
                                      av[:, :, 64:128], start=False, stop=True)
                wb = [P + "s_a16",
                      (P + f"v_supd{cc - 1}") if cc else w_("v_supd3", 1)]
                S.add(B1, "T", op_bac, waits=[w for w in wb if w],
                      mark=P + f"t_bac{cc}")

            for pi in range(2):
                pr = 64 * pi
                ccA, ccB = 2 * pi, 2 * pi + 1
                S_A, S_B = sel[0], sel[1]

                def op_oi(eng, pr=pr):
                    last = None
                    for h in range(16):
                        g, hh = divmod(h, 8)
                        last = eng.matmul(
                            pP[pr:pr + 64, h * 64:(h + 1) * 64],
                            PTM[pr:pr + 64, g * 512 + hh * 64:g * 512 + (hh + 1) * 64],
                            VT16[pr:pr + 64, h * 64:(h + 1) * 64],
                            start=(h in (0, 8)), stop=False)
                    return last
                woi = [P + f"v_ptm{pi}1", P + "v_vt"]
                if w_("v_oevac", 1):
                    woi.append(w_("v_oevac", 1))
                S.add(B1, "T", op_oi, waits=woi, mark=P + f"t_oi{pi}")

                def op_oeA(eng, ccA=ccA, S_A=S_A):
                    last = None
                    for h in (0, 2, 4, 6, 8, 10, 12, 14, None,
                              1, 3, 5, 7, 9, 11, 13, 15):
                        if h is None:
                            tdrain(eng, last)
                            continue
                        pb = 64 * (h % 2)
                        last = eng.matmul(
                            pP[ccA * 32:ccA * 32 + 32, h * 64:(h + 1) * 64],
                            QA[pb:pb + 64, h // 2, ccA * 32:ccA * 32 + 32],
                            S_A[pb:pb + 64, 64 * (h // 2):64 * (h // 2) + 64],
                            start=False, stop=False)
                    return last
                woa = [P + f"t_oi{pi}"]
                if pi:
                    woa.append(P + "v_supd1")
                elif w_("v_supd3", 1):
                    woa.append(w_("v_supd3", 1))
                S.add(B1, "T", op_oeA, waits=woa, mark=P + f"t_oeA{pi}")

                add_bac(ccA)
                def op_uA(eng, ccA=ccA):
                    last = None
                    for h in range(16):
                        pb = 64 * (h % 2)
                        last = eng.matmul(
                            pU[pb:pb + 64, 64 * (h // 2):64 * (h // 2) + 64],
                            QKB[ccA * 32:(ccA + 1) * 32, 1024 + h * 64:1024 + (h + 1) * 64],
                            VT16[ccA * 32:(ccA + 1) * 32, h * 64:(h + 1) * 64],
                            start=True, stop=True)
                    return last
                wuA = [P + "v_vt",
                       (P + "v_supd1") if pi else w_("v_supd3", 1)]
                S.add(B1, "T", op_uA, waits=[w for w in wuA if w],
                      mark=P + f"t_u{ccA}",
                      drains=[P + f"t_bac{ccA}"] if ccA == 2 else ())
                S.add(B1, "V", lambda eng, S_A=S_A, S_B=S_B: (
                    eng.tensor_tensor(SCR_S[:, :], S_A[:, :], pU[:, :], ALU.add),
                    eng.tensor_tensor(S_B[:, :], SCR_S[:, :], pBAC[:, :], ALU.mult))[-1],
                    waits=[P + f"t_u{ccA}", P + f"t_bac{ccA}", P + f"t_oeA{pi}"],
                    mark=P + f"v_supd{ccA}")

                add_bac(ccB)
                def op_oeB(eng, pr=pr, S_B=S_B):
                    last = None
                    for h in (0, 2, 4, 6, 8, 10, 12, 14, None,
                              1, 3, 5, 7, 9, 11, 13, 15):
                        if h is None:
                            tdrain(eng, last)
                            continue
                        pb = 64 * (h % 2)
                        last = eng.matmul(
                            pP[pr:pr + 64, h * 64:(h + 1) * 64],
                            QAzO[pb:pb + 64, h // 2, pr:pr + 64],
                            S_B[pb:pb + 64, 64 * (h // 2):64 * (h // 2) + 64],
                            start=False, stop=(h in (7, 15)))
                    return last
                S.add(B1, "T", op_oeB,
                      waits=[P + f"v_supd{ccA}", P + "v_qaz"],
                      mark=P + f"t_oeB{pi}")

                def op_uB(eng, ccB=ccB):
                    last = None
                    for h in range(16):
                        pb = 64 * (h % 2)
                        if ccB == 3:
                            last = eng.matmul(
                                pU[pb:pb + 64, 64 * (h // 2):64 * (h // 2) + 64],
                                QKB[64:128, 1024 + h * 64:1024 + (h + 1) * 64],
                                VT16z[64:128, h * 64:(h + 1) * 64],
                                start=True, stop=True)
                        else:
                            last = eng.matmul(
                                pU[pb:pb + 64, 64 * (h // 2):64 * (h // 2) + 64],
                                QKB[ccB * 32:(ccB + 1) * 32,
                                    1024 + h * 64:1024 + (h + 1) * 64],
                                VT16[ccB * 32:(ccB + 1) * 32, h * 64:(h + 1) * 64],
                                start=True, stop=True)
                    return last
                wuB = [P + "v_vt", P + f"v_supd{ccA}"]
                if ccB == 3:
                    wuB.append(P + "v_vtz")
                S.add(B1, "T", op_uB, waits=wuB, mark=P + f"t_u{ccB}",
                      drains=[P + f"t_oeB{pi}"] if ccB == 1 else ())
                S.add(B1, "V", lambda eng, S_A=S_A, S_B=S_B: (
                    eng.tensor_tensor(SCR_S[:, :], S_B[:, :], pU[:, :], ALU.add),
                    eng.tensor_tensor(S_A[:, :], SCR_S[:, :], pBAC[:, :], ALU.mult))[-1],
                    waits=[P + f"t_u{ccB}", P + f"t_bac{ccB}", P + f"t_oeB{pi}"],
                    mark=P + f"v_supd{ccB}")

            if debug and it <= 1:
                S.add(B1, "G", lambda eng, dma, s=s, t=t: dma(
                    dbg["dbg_s0"][:, :], SST[s][(t * 4 + 4) % 2][:, :]),
                    waits=[P + "v_supd3"], dmas=1, dsem="ddbg")

            S.add(B1, "V", lambda eng: eng.tensor_tensor(
                OB16[:, :], pP[:, 0:1024], A16[:, :], ALU.mult),
                waits=[P + "t_oeB1", P + "s_a16", w_("t_ot", 1)], mark=P + "v_oevac")

            def op_ot(eng):
                last = None
                for k in range(8):
                    last = eng.transpose(pTR[:, k * 128:(k + 1) * 128],
                                         OB16[:, k * 128:(k + 1) * 128], cst["idn16"][:, :])
                return last
            S.add(B1, "T", op_ot, waits=[P + "v_oevac", P + "s_ka"], mark=P + "t_ot")
            S.add(B1, "S", lambda eng: eng.copy(OA16[:, :, :], pTR[:, 0:1024]),
                  waits=[P + "t_ot"], mark=P + "s_oa")

            def op_wo(eng):
                last = None
                for k in range(8):
                    for nb in range(2):
                        last = eng.matmul(pP[:, 1024 + nb * 512:1024 + (nb + 1) * 512],
                                          OA16[:, k, :], WO[:, k, nb * 512:(nb + 1) * 512],
                                          start=(k == 0), stop=(k == 7))
                return last
            S.add(B1, "T", op_wo, waits=[P + "s_oa", P + "s_arec", P + "s_a16"],
                  mark=P + "t_wo")
            S.add(B1, "S", lambda eng: eng.copy(sB16[:, :], pP[:, 1024:2048]),
                  waits=[P + "t_wo", w_("g_s", 1)], mark=P + "v_sevac")
            S.add(B1, "G", lambda eng, dma, t=t, s=s: dma(
                d_s_scr[s, t * T:(t + 1) * T, :], sB16[:, :]),
                waits=[P + "v_sevac"], dmas=1, mark=P + "g_s", dsem="dsts")
            if debug and it == 0:
                S.add(B1, "G", lambda eng, dma: dma(dbg["dbg_sout"][:, :], sB16[:, :]),
                      waits=[P + "v_sevac"], dmas=1, dsem="ddbg")

            def op_q1(eng, ib=ib, cols=cols):
                return eng.tensor_scalar(T0[:, :], xB[ib][:, :], cols[:, 10:11],
                                         cols[:, 11:12], ALU.mult, ALU.add)
            S.add(B1, "V", op_q1, waits=[P + "qcols", w_("g_q2", 1)], mark=P + "v_q1")
            S.add(B1, "S", lambda eng: eng.activation(
                XQ1B[:, :], T0[:, :], AF.Sign, bias=0.0, scale=1.0),
                waits=[P + "v_q1", w_("t_xqt", 1)], mark=P + "s_sgn1")
            def op_q2(eng):
                eng.tensor_tensor(T0[:, :], T0[:, :], XQ1B[:, :], ALU.mult)
                return eng.tensor_copy(ICST[:, :], T0[:, :])
            S.add(B1, "V", op_q2, waits=[P + "s_sgn1"], mark=P + "g_q2")
            def op_q3(eng):
                eng.tensor_tensor(T0[:, :], T0[:, :], ICST[:, :], ALU.subtract)
                eng.tensor_scalar(T0[:, :], T0[:, :], 0.5, None, ALU.is_ge)
                eng.tensor_tensor(T0[:, :], T0[:, :], ICST[:, :], ALU.add)
                eng.tensor_scalar(T0[:, :], T0[:, :], 127.0, None, ALU.min)
                return eng.tensor_tensor(XQ1B[:, :], XQ1B[:, :], T0[:, :], ALU.mult)
            S.add(B1, "V", op_q3, waits=[P + "g_q2"], mark=P + "v_q3")
            if debug and it == 0:
                S.add(B1, "G", lambda eng, dma: dma(dbg["dbg_xq"][:, :], XQ1B[:, :]),
                      waits=[P + "v_q3"], dmas=1, dsem="ddbg")

            def op_xqt(eng):
                last = None
                for k in range(8):
                    last = eng.transpose(pTR[:, k * 128:(k + 1) * 128],
                                         XQ1B[:, k * 128:(k + 1) * 128], cst["idn16"][:, :])
                return last
            S.add(B1, "T", op_xqt, waits=[P + "v_q3", P + "s_oa"], mark=P + "t_xqt")
            S.add(B1, "S", lambda eng: eng.copy(XQ1A[:, :, :], pTR[:, 0:1024]),
                  waits=[P + "t_xqt", w_("g_xq", 1)], mark=P + "s_xqa")
            S.add(B1, "G", lambda eng, dma, t=t, s=s, cols=cols: (
                dma(d_xq_scr[s, t, :, :, :], XQ1A[:, :, :]),
                dma(d_s1_scr[s, t, :, :], cols[:, 9:10])),
                waits=[P + "s_xqa", P + "qcols"], dmas=2, mark=P + "g_xq", dsem="dstxq")

            if it < n_stat_iters:
                for k in range(stats_per_iter):
                    ci = it * stats_per_iter + k
                    if ci >= len(wpieces):
                        continue
                    src, _ = wp_src(wpieces[ci])
                    wb = ci % 2
                    S.add(B1, "Y", lambda eng, dma, src=src, wb=wb: dma(WCH[wb][:, :], src),
                          waits=[f"wstat_{ci - 2}"] if ci >= 2 else ["c_ld"],
                          dmas=1, mark=f"wld_{ci}", dsem=f"dwl{ci % 2}")
                    S.add(B1, "S", lambda eng, wb=wb, ci=ci: eng.activation(
                        SCRE[:, :], WCH[wb][:, :], AF.Abs, bias=0.0, scale=1.0,
                        accum_out=abscols[:, ci % 64:ci % 64 + 1]),
                        waits=[f"wld_{ci}", "v_sinit"], mark=f"wstat_{ci}")

            if it == fin_iter:
                def op_wmean(eng):
                    eng.tensor_reduce(abscols[:, 72:73], abscols[:, 0:32], AX.X, ALU.add)
                    return eng.tensor_reduce(abscols[:, 73:74], abscols[:, 32:64], AX.X, ALU.add)
                S.add(B1, "V", op_wmean, waits=[f"wstat_{len(wpieces) - 1}"], mark="v_wmean")
                def op_wtot(eng):
                    eng.matmul(pPT[0:1, 256:257], cst["ones_col32"][:, :], abscols[:, 72:73],
                               start=True, stop=True)
                    return eng.matmul(pPT[0:1, 257:258], cst["ones_col32"][:, :],
                                      abscols[:, 73:74], start=True, stop=True)
                S.add(B1, "T", op_wtot, waits=["v_wmean", P + "v_ptm11"], mark="t_wtot")
                def op_wsc(eng):
                    eng.tensor_scalar(smalls[0:1, 0:1], pPT[0:1, 256:257],
                                      1.0 / (DH4 * D), None, ALU.mult)
                    eng.tensor_scalar(smalls[0:1, 1:2], pPT[0:1, 257:258],
                                      1.0 / (DH4 * D), None, ALU.mult)
                    eng.tensor_scalar(smalls[0:1, 2:3], smalls[0:1, 0:1], 1e-5, None, ALU.max)
                    eng.tensor_scalar(smalls[0:1, 3:4], smalls[0:1, 1:2], 1e-5, None, ALU.max)
                    eng.reciprocal(smalls[0:1, 4:5], smalls[0:1, 2:3])
                    eng.reciprocal(smalls[0:1, 5:6], smalls[0:1, 3:4])
                    for rc, dc_ in ((4, 2), (5, 3)):
                        eng.tensor_tensor(smalls[0:1, 6:7], smalls[0:1, dc_:dc_ + 1],
                                          smalls[0:1, rc:rc + 1], ALU.mult)
                        eng.tensor_scalar(smalls[0:1, 6:7], smalls[0:1, 6:7], -1.0, 2.0,
                                          ALU.mult, ALU.add)
                        eng.tensor_tensor(smalls[0:1, rc:rc + 1], smalls[0:1, rc:rc + 1],
                                          smalls[0:1, 6:7], ALU.mult)
                    return eng.tensor_copy(smalls[0:1, 7:8], smalls[0:1, 5:6])
                S.add(B1, "V", op_wsc, waits=["t_wtot"], mark="v_wsc")
                def op_wscb(eng):
                    eng.matmul(pPT[:, 256:257], cst["ones_row32"][:, :], smalls[0:1, 4:5],
                               start=True, stop=True)
                    eng.matmul(pPT[:, 257:258], cst["ones_row32"][:, :], smalls[0:1, 2:3],
                               start=True, stop=True)
                    eng.matmul(pPT[:, 258:259], cst["ones_row32"][:, :], smalls[0:1, 5:6],
                               start=True, stop=True)
                    return eng.matmul(pPT[:, 259:260], cst["ones_row32"][:, :],
                                      smalls[0:1, 3:4], start=True, stop=True)
                S.add(B1, "T", op_wscb, waits=["v_wsc"], mark="t_wscb")
                S.add(B1, "V", lambda eng: eng.tensor_copy(swcols[:, 0:4], pPT[:, 256:260]),
                      waits=["t_wscb"], mark="v_swcols")

            if it >= q_start and it > fin_iter - 1:
                for k in range(quant_per_iter):
                    ci = (it - q_start) * quant_per_iter + k
                    if ci >= len(wpieces):
                        continue
                    src, dstq = wp_src(wpieces[ci])
                    scol = swcols[:, 0:1] if wpieces[ci][0] == "w1" else swcols[:, 2:3]
                    tag = f"wq{ci}"
                    S.add(B1, "Y", lambda eng, dma, src=src: dma(WCH[0][:, :], src),
                          waits=[last_q_mark[0] or "v_swcols"], dmas=1, mark=tag + "ld", dsem="dwl0")
                    def op_qv1(eng, scol=scol):
                        return eng.tensor_scalar(WT0[:, :], WCH[0][:, :], scol,
                                                 None, ALU.mult)
                    S.add(B1, "V", op_qv1, waits=[tag + "ld", "v_swcols"], mark=tag + "v1")
                    S.add(B1, "S", lambda eng: eng.activation(
                        WQ16[:, :], WT0[:, :], AF.Sign, bias=0.0, scale=1.0),
                        waits=[tag + "v1", (last_q_mark[0] or "v_swcols")],
                        mark=tag + "sg")
                    def op_qg(eng):
                        eng.tensor_tensor(WT0[:, :], WT0[:, :], WQ16[:, :], ALU.mult)
                        return eng.tensor_copy(WIC[:, :], WT0[:, :])
                    S.add(B1, "V", op_qg, waits=[tag + "sg"], mark=tag + "g")
                    def op_qv2(eng):
                        eng.tensor_tensor(WT0[:, :], WT0[:, :], WIC[:, :], ALU.subtract)
                        eng.tensor_scalar(WT0[:, :], WT0[:, :], 0.5, None, ALU.is_ge)
                        eng.tensor_tensor(WT0[:, :], WT0[:, :], WIC[:, :], ALU.add)
                        eng.tensor_scalar(WT0[:, :], WT0[:, :], 1.0, None, ALU.min)
                        return eng.tensor_tensor(WQ16[:, :], WQ16[:, :], WT0[:, :],
                                                 ALU.mult)
                    S.add(B1, "V", op_qv2, waits=[tag + "g"], mark=tag + "v2")
                    S.add(B1, "G", lambda eng, dma, dstq=dstq: dma(dstq, WQ16[:, :]),
                          waits=[tag + "v2"], dmas=1, mark=tag + "st", dsem="dstwq")
                    last_q_mark[0] = tag + "st"
        assert last_q_mark[0] == f"wq{len(wpieces) - 1}st", "weight quant incomplete"

        S.ops = [o for o in S.ops if o.block <= max_block]
        import os
        _b1lim = int(os.environ.get("B1_PREFIX", "0"))
        if _b1lim:
            _b1 = [o for o in S.ops if o.block == 1]
            _keep = set(id(o) for o in _b1[:_b1lim])
            S.ops = [o for o in S.ops if o.block != 1 or id(o) in _keep]
        S.plan()
        for bi in (0, 1):
            engs = sorted({o.eng for o in S.ops if o.block == bi})
            if not engs:
                continue
            with nc.Block() as blk:
                regs = {"Y": blk.sync, "T": blk.tensor, "V": blk.vector,
                        "S": blk.scalar, "G": blk.gpsimd}
                for ek in engs:
                    def make(ek=ek, bi=bi):
                        def f(eng):
                            S.emit_stream(ek, eng, sems, bi, junk=junkc)
                        return f
                    regs[ek](make())
        ctxA.close()
        S1_counts = dict(S.counts)
        S = Sched(start=S1_counts)
        for _nm in ("dsts", "dstxq", "dstwq"):
            S.marks["A_" + _nm] = (_nm, S1_counts.get(_nm, 0))

        B2 = 2
        ctxB = contextlib.ExitStack()
        allocB = lambda *a: ctxB.enter_context(nc.sbuf_tensor(*a))
        pallocB = lambda *a: ctxB.enter_context(nc.psum_tensor(*a))
        w1qT = allocB("w1qT", [128, 8, DH4], BF16)
        for name in _B_ONLY:
            cst[name] = allocB(name + "_sb", list(_CONST[name].shape), F32)
        def op_ldshifts(eng, dma):
            for name in sorted(_B_ONLY):
                dma(cst[name][:, :], dc[name][:, :])
        S.add(B2, "Y", op_ldshifts, dmas=3, mark="shld", dsem="dshl")
        cw0B = allocB("cw0B", [128, DH4], F32)
        cw1B = allocB("cw1B", [128, DH4], F32)
        cw2B = allocB("cw2B", [128, DH4], F32)
        def op_cwb(eng, dma):
            for si, dst in enumerate((cw0B, cw1B, cw2B)):
                row = d_cwT[si:si + 1, :]
                ap = bass.AP(row.tensor, row.offset,
                             [[0, 128]] + [list(p) for p in row.ap[1:]])
                dma(dst[:, :], ap)
            dma(cbrow[:, :], d_cb[:, :])
        S.add(B2, "G", op_cwb, dmas=4, mark="cwb", dsem="dcwb")
        def op_w1t(eng, dma):
            for dch in range(8):
                dma(w1qT[:, dch, :], d_w1q_scr[:, dch * 128:(dch + 1) * 128], transpose=True)
        S.add(B2, "Y", op_w1t, waits=["A_dstwq"], dmas=8, mark="w1t", dsem="dw1t")

        B3 = 3
        h0 = allocB("h0", [128, DH4], F32)
        h1 = allocB("h1", [128, DH4], F32)
        h2 = allocB("h2", [128, DH4], F32)
        halos = allocB("halos", [4, DH4], F32)
        cbrow = halos[3:4, :]
        ySf = allocB("ySf", [128, DH4], F32)
        y2s16 = allocB("y2s16", [128, 1024], BF16)
        Y2SA = allocB("Y2SA", [128, 8, 128], BF16)
        XQA = allocB("XQA", [128, 8, 128], BF16)
        s1c = allocB("s1c", [128, 2], F32)
        QIC = allocB("QIC", [128, 512], I32)
        QT = allocB("QT", [128, 512], F32)
        qcol = allocB("qcol", [128, 8], F32)
        pBH = pallocB("pBH", [128, 2048], F32)
        pBT = pallocB("pBT", [128, 1024], BF16)

        for s_ in range(NSEQ):
            for t in range(n_tiles + 1):
                jj = s_ * (n_tiles + 1) + t
                P = f"a{jj}_"
                Pp = f"a{jj - 1}_"
                do_f = t < n_tiles

                if t == 0:
                    S.add(B3, "V", lambda eng: eng.memset(halos[0:3, :], 0.0),
                          waits=[Pp + "conv3"] if jj else [], mark=P + "hz")

                if do_f:
                    def op_ldxq(eng, dma, t=t, s_=s_):
                        dma(XQA[:, :, :], d_xq_scr[s_, t, :, :, :])
                        dma(s1c[:, 0:1], d_s1_scr[s_, t, :, :])
                    wq_ = ["w1t", "A_dstxq"]
                    if jj:
                        pf = jj - 1 if t else jj - 2
                        if pf >= 0:
                            wq_ += [f"a{pf}_ffn1_3", f"a{pf}_hev3"]
                    S.add(B3, "Y", op_ldxq, waits=wq_, dmas=2, mark=P + "dxq", dsem="dbxq")
                    S.add(B3, "V", lambda eng: eng.scalar_tensor_tensor(
                        s1c[:, 1:2], s1c[:, 0:1], 1.0, swcols[:, 1:2],
                        ALU.mult, ALU.mult), waits=[P + "dxq"], mark=P + "s1tot")
                    for q in range(4):
                        qb = q % 2
                        wf = [P + "dxq"]
                        if q >= 2:
                            wf.append(P + f"hev{q - 2}")
                        elif jj:
                            wf.append(Pp + f"silu{q + 2}")
                        def op_ffn1(eng, q=q, qb=qb):
                            last = None
                            for k in range(8):
                                for nb in range(2):
                                    last = eng.matmul(
                                        pBH[:, qb * 1024 + nb * 512:qb * 1024 + (nb + 1) * 512],
                                        XQA[:, k, :],
                                        w1qT[:, k, q * 1024 + nb * 512:q * 1024 + (nb + 1) * 512],
                                        start=(k == 0), stop=(k == 7))
                            return last
                        S.add(B3, "T", op_ffn1, waits=wf, mark=P + f"ffn1_{q}")
                        whb = [P + f"ffn1_{q}", P + "s1tot"]
                        if jj:
                            whb.append(Pp + f"silu{q}")
                        S.add(B3, "S", lambda eng, q=q, qb=qb: eng.activation(
                            h2[:, q * 1024:(q + 1) * 1024],
                            pBH[:, qb * 1024:(qb + 1) * 1024],
                            AF.Identity, bias=0.0, scale=s1c[:, 1:2]),
                            waits=whb, mark=P + f"hb{q}")
                        whev = [P + f"hb{q}"]
                        if jj == 0 and q == 0:
                            whev.append("cwb")
                        if jj and q == 0:
                            whev.append(Pp + "conv3")
                            if t:
                                whev.append(Pp + "stg")
                        def op_hevg(eng, q=q):
                            sl = slice(q * 1024, (q + 1) * 1024)
                            eng.tensor_tensor(h0[:, sl], h2[:, sl], cw0B[:, sl],
                                              ALU.mult)
                            return eng.tensor_tensor(h1[:, sl], h2[:, sl],
                                                     cw1B[:, sl], ALU.mult)
                        S.add(B3, "G", op_hevg, waits=whev, mark=P + f"hevg{q}")
                        S.add(B3, "V", lambda eng, q=q: eng.tensor_tensor(
                            h2[:, q * 1024:(q + 1) * 1024],
                            h2[:, q * 1024:(q + 1) * 1024],
                            cw2B[:, q * 1024:(q + 1) * 1024], ALU.mult),
                            waits=[P + f"hevg{q}"], mark=P + f"hev{q}")

                for q in range(4):
                    qb = q % 2
                    wcv = ["cwb", "shld"] if jj == 0 and q == 0 else []
                    if t == 0:
                        wcv.append(P + "hz")
                    if do_f:
                        if q < 2:
                            wcv.append(P + f"hev{q + 2}")
                            wcv.append(P + "hev3")
                        else:
                            wcv.append(P + f"silu{q - 2}")
                            wcv.append(P + "hev3")
                    else:
                        if jj:
                            wcv.append(Pp + "stg")
                        if q >= 2:
                            wcv.append(P + f"silu{q - 2}")
                        elif jj:
                            wcv.append(Pp + f"cadd{q + 2}")
                    if t and q == 0:
                        wcv.append(Pp + "stg")
                    def op_conv(eng, q=q, qb=qb):
                        last = None
                        for nb in range(2):
                            ps = pBH[:, qb * 1024 + nb * 512:qb * 1024 + (nb + 1) * 512]
                            o0 = q * 1024 + nb * 512
                            eng.matmul(ps, cst["sh2_r"][:, :], h0[:, o0:o0 + 512],
                                       start=True, stop=False)
                            eng.matmul(ps, cst["sh1_r"][:, :], h1[:, o0:o0 + 512],
                                       start=False, stop=False)
                            last = eng.matmul(ps, cst["erows_h"][0:4, :],
                                              halos[0:4, o0:o0 + 512],
                                              start=False, stop=True)
                        return last
                    S.add(B3, "T", op_conv, waits=[w for w in wcv if w],
                          mark=P + f"conv{q}")
                    if do_f:
                        S.add(B3, "V", lambda eng, q=q, qb=qb: eng.tensor_tensor(
                            h2[:, q * 1024:(q + 1) * 1024],
                            h2[:, q * 1024:(q + 1) * 1024],
                            pBH[:, qb * 1024:(qb + 1) * 1024], ALU.add),
                            waits=[P + f"conv{q}", P + f"hev{q}"],
                            mark=P + f"cadd{q}")
                        S.add(B3, "S", lambda eng, q=q: eng.activation(
                            ySf[:, q * 1024:(q + 1) * 1024],
                            h2[:, q * 1024:(q + 1) * 1024],
                            AF.Silu, bias=0.0, scale=1.0),
                            waits=[P + f"cadd{q}"] +
                                  ([Pp + f"qb{2 * q + 1}"] if jj else []),
                            mark=P + f"silu{q}")
                    else:
                        S.add(B3, "S", lambda eng, q=q, qb=qb: eng.activation(
                            ySf[:, q * 1024:(q + 1) * 1024],
                            pBH[:, qb * 1024:(qb + 1) * 1024],
                            AF.Silu, bias=0.0, scale=1.0),
                            waits=[P + f"conv{q}"] +
                                  ([Pp + f"qb{2 * q + 1}"] if jj else []),
                            mark=P + f"silu{q}")
                    S.add(B3, "V", lambda eng, q=q: eng.reduce_max(
                        qcol[:, q:q + 1], ySf[:, q * 1024:(q + 1) * 1024], AX.X,
                        apply_absolute_value=True),
                        waits=[P + f"silu{q}"] + ([Pp + "qm2"] if jj and q == 0 else []),
                        mark=P + f"qmx{q}")

                if do_f:
                    def op_stg(eng, dma):
                        dma(halos[0:1, 0:DH4], h0[126:127, :])
                        dma(halos[1:2, 0:DH4], h1[127:128, :])
                        dma(halos[2:3, 0:DH4], h0[127:128, :])
                    S.add(B3, "G", op_stg, waits=[P + "conv3", P + "hev3"],
                          dmas=3, mark=P + "stg", dsem="dhalo")

                def op_qm2(eng):
                    eng.tensor_tensor(qcol[:, 0:1], qcol[:, 0:1], qcol[:, 1:2], ALU.max)
                    eng.tensor_tensor(qcol[:, 2:3], qcol[:, 2:3], qcol[:, 3:4], ALU.max)
                    eng.tensor_tensor(qcol[:, 0:1], qcol[:, 0:1], qcol[:, 2:3], ALU.max)
                    eng.tensor_scalar(qcol[:, 1:2], qcol[:, 0:1], 1e-5, None, ALU.max)
                    eng.reciprocal(qcol[:, 2:3], qcol[:, 1:2])
                    eng.tensor_tensor(qcol[:, 6:7], qcol[:, 1:2], qcol[:, 2:3], ALU.mult)
                    eng.tensor_scalar(qcol[:, 6:7], qcol[:, 6:7], -1.0, 2.0,
                                      ALU.mult, ALU.add)
                    eng.tensor_tensor(qcol[:, 2:3], qcol[:, 2:3], qcol[:, 6:7], ALU.mult)
                    eng.tensor_scalar(qcol[:, 3:4], qcol[:, 2:3], 127.0, None, ALU.mult)
                    eng.tensor_scalar(qcol[:, 4:5], qcol[:, 1:2], 1.0 / 127.0, None, ALU.mult)
                    return eng.scalar_tensor_tensor(qcol[:, 5:6], qcol[:, 4:5], 1.0,
                                                    swcols[:, 3:4], ALU.mult, ALU.mult)
                S.add(B3, "V", op_qm2, waits=[P + "qmx0", P + "qmx1", P + "qmx2",
                                              P + "qmx3"], mark=P + "qm2")
                for qq in range(4):
                    for hh in range(2):
                        seg = qq * 2 + hh
                        sl = slice(seg * 512, (seg + 1) * 512)
                        osl = slice(hh * 512, (hh + 1) * 512)
                        wsg = [P + f"silu{seg // 2}"]
                        if hh == 0:
                            if qq:
                                wsg.append(P + f"y2t{qq - 1}")
                            elif jj:
                                wsg.append(Pp + "y2t3")
                        S.add(B3, "S", lambda eng, sl=sl, osl=osl: eng.activation(
                            y2s16[:, osl], ySf[:, sl], AF.Sign, bias=0.0, scale=1.0),
                            waits=wsg, mark=P + f"qs{seg}")
                        wqb = [P + "qm2", P + f"qs{seg}"]
                        if seg:
                            wqb.append(P + f"qd{seg - 1}")
                        elif jj:
                            wqb.append(Pp + "qd7")
                        S.add(B3, "V", lambda eng, sl=sl, osl=osl:
                              eng.scalar_tensor_tensor(QT[:, :], ySf[:, sl],
                                                       qcol[:, 3:4], y2s16[:, osl],
                                                       ALU.mult, ALU.mult),
                              waits=wqb, mark=P + f"qb{seg}")
                        S.add(B3, "G", lambda eng: eng.tensor_copy(QIC[:, :], QT[:, :]),
                              waits=[P + f"qb{seg}"], mark=P + f"qc{seg}")
                        S.add(B3, "V", lambda eng, sl=sl, osl=osl: (
                            eng.tensor_tensor(QT[:, :], QT[:, :], QIC[:, :],
                                              ALU.subtract),
                            eng.scalar_tensor_tensor(QT[:, :], QT[:, :], 0.5,
                                                     QIC[:, :], ALU.is_ge, ALU.add),
                            eng.scalar_tensor_tensor(y2s16[:, osl], QT[:, :], 127.0,
                                                     y2s16[:, osl], ALU.min,
                                                     ALU.mult))[-1],
                            waits=[P + f"qc{seg}"], mark=P + f"qd{seg}")
                    def op_y2t(eng):
                        last = None
                        for k in range(8):
                            last = eng.transpose(pBT[:, k * 128:(k + 1) * 128],
                                                 y2s16[:, k * 128:(k + 1) * 128],
                                                 cst["idn16"][:, :])
                        return last
                    wy = [P + f"qd{2 * qq + 1}"]
                    if qq:
                        wy.append(P + f"y2a{qq - 1}")
                    elif jj:
                        wy.append(Pp + "y2a3")
                    S.add(B3, "T", op_y2t, waits=wy, mark=P + f"y2t{qq}")
                    wy2a = [P + f"y2t{qq}",
                            (P + f"y2st{qq - 1}") if qq else
                            ((Pp + "y2st3") if jj else None)]
                    S.add(B3, "S", lambda eng: eng.copy(Y2SA[:, :, :], pBT[:, 0:1024]),
                          waits=[w for w in wy2a if w], mark=P + f"y2a{qq}")
                    def op_y2st(eng, dma, qq=qq, t=t, s_=s_):
                        dma(d_y2a_scr[s_, t, :, qq * 8:(qq + 1) * 8, :], Y2SA[:, :, :])
                        if qq == 0:
                            dma(d_s2_scr[s_, t, :, :], qcol[:, 5:6])
                    S.add(B3, "G", op_y2st,
                          waits=[P + f"y2a{qq}"], dmas=2 if qq == 0 else 1,
                          mark=P + f"y2st{qq}", dsem="dy2a")

        S.ops = [o for o in S.ops if o.block <= max_block]
        _b3lim = int(os.environ.get("B3_PREFIX", "0"))
        if _b3lim:
            _b3 = [o for o in S.ops if o.block == 3]
            _keep = set(id(o) for o in _b3[:_b3lim])
            S.ops = [o for o in S.ops if o.block != 3 or id(o) in _keep]
        S.plan()
        for bi in (2, 3):
            engs = sorted({o.eng for o in S.ops if o.block == bi})
            if engs:
                with nc.Block() as blk:
                    regs = {"Y": blk.sync, "T": blk.tensor, "V": blk.vector,
                            "S": blk.scalar, "G": blk.gpsimd}
                    for ek in engs:
                        def make(ek=ek, bi=bi):
                            def f(eng):
                                S.emit_stream(ek, eng, sems, bi, junk=junkc)
                            return f
                        regs[ek](make())
        ctxB.close()
        B3_counts = dict(S.counts)
        S = Sched(start=B3_counts)
        for _nm in ("dsts", "dstwq", "dy2a"):
            S.marks["B_" + _nm] = (_nm, B3_counts.get(_nm, 0))

        B4 = 4
        w2qT = alloc("w2qT", [128, 32, 1024], BF16)
        s2c = alloc("s2c", [128, 2], F32)
        Y2L = [alloc(f"Y2L{i}", [128, 32, 128], BF16) for i in range(2)]
        xBb = [alloc(f"xBb{i}", [128, D], F32) for i in range(2)]
        sBb = [alloc(f"sBb{i}", [128, D], BF16) for i in range(2)]
        OUTS = [alloc(f"OUTS{i}", [128, D], F32) for i in range(2)]
        pBG = palloc("pBG", [128, 1024], F32)

        for wg_ in range(4):
            def op_w2t(eng, dma, wg_=wg_):
                for k in range(wg_ * 8, (wg_ + 1) * 8):
                    dma(w2qT[:, k, :], d_w2q_scr[:, k * 128:(k + 1) * 128],
                        transpose=True)
            S.add(B4, "Y", op_w2t, waits=["B_dstwq"] if wg_ == 0 else [],
                  dmas=8, mark=f"w2t{wg_}", dsem="dw2t")
        S.add(B4, "V", lambda eng: (
            eng.memset(xBb[0][:, :], 0.0), eng.memset(xBb[1][:, :], 0.0),
            eng.memset(sBb[0][:, :], 0.0), eng.memset(sBb[1][:, :], 0.0))[-1],
            mark="b4z")

        for s_ in range(NSEQ):
            for t in range(n_tiles + 1):
                jj = s_ * (n_tiles + 1) + t
                P = f"c{jj}_"
                Pp = f"c{jj - 1}_"
                Ppp = f"c{jj - 2}_"
                jb = jj % 2
                r0 = t * T - 1
                lo = 1 if t == 0 else 0
                hi = 1 if t == n_tiles else 128
                def op_ld(eng, dma, t=t, s_=s_, jb=jb, r0=r0, lo=lo, hi=hi):
                    dma(Y2L[jb][:, :, :], d_y2a_scr[s_, t, :, :, :])
                    dma(s2c[:, jb:jb + 1], d_s2_scr[s_, t, :, :])
                    dma(xBb[jb][lo:hi, :], d_x[s_, r0 + lo:r0 + hi, :])
                    dma(sBb[jb][lo:hi, :], d_s_scr[s_, r0 + lo:r0 + hi, :])
                wl = ["B_dy2a", "B_dsts", "b4z"]
                if jj >= 2:
                    wl += [Ppp + "ffn2_31", Ppp + "comb"]
                S.add(B4, "Y", op_ld, waits=wl, dmas=4, mark=P + "ld",
                      dsem=f"dbl{jb}")
                for k in range(32):
                    def op_ffn2(eng, k=k, jb=jb):
                        last = None
                        for nb in range(2):
                            last = eng.matmul(pBG[:, nb * 512:(nb + 1) * 512],
                                              Y2L[jb][:, k, :],
                                              w2qT[:, k, nb * 512:(nb + 1) * 512],
                                              start=(k == 0), stop=(k == 31))
                        return last
                    wf = [P + "ld"]
                    if jj == 0 and k % 8 == 0:
                        wf.append(f"w2t{k // 8}")
                    if k == 0 and jj:
                        wf.append(Pp + "gsilu")
                    S.add(B4, "T", op_ffn2, waits=wf,
                          mark=(P + f"ffn2_{k}") if k in (0, 31) else None)
                wgs = [P + "ffn2_31"]
                if jj >= 2:
                    wgs.append(Ppp + "outst")
                S.add(B4, "S", lambda eng, jb=jb: eng.activation(
                    OUTS[jb][:, :], pBG[:, :], AF.Silu, bias=0.0,
                    scale=s2c[:, jb:jb + 1]),
                    waits=wgs, mark=P + "gsilu")
                def op_comb(eng, jb=jb):
                    eng.tensor_tensor(OUTS[jb][:, :], OUTS[jb][:, :], sBb[jb][:, :],
                                      ALU.mult)
                    return eng.tensor_tensor(OUTS[jb][:, :], OUTS[jb][:, :],
                                             xBb[jb][:, :], ALU.add)
                S.add(B4, "V", op_comb, waits=[P + "gsilu", P + "ld"], mark=P + "comb")
                S.add(B4, "G", lambda eng, dma, t=t, s_=s_, jb=jb, r0=r0, lo=lo, hi=hi:
                      dma(d_out[s_, r0 + lo:r0 + hi, :], OUTS[jb][lo:hi, :]),
                      waits=[P + "comb"], dmas=1, mark=P + "outst",
                      dsem=f"dout{jb}")

        S.ops = [o for o in S.ops if o.block <= max_block]
        S.plan()
        for bi in (4,):
            engs = sorted({o.eng for o in S.ops if o.block == bi})
            if not engs:
                continue
            with nc.Block() as blk:
                regs = {"Y": blk.sync, "T": blk.tensor, "V": blk.vector,
                        "S": blk.scalar, "G": blk.gpsimd}
                for ek in engs:
                    def make(ek=ek, bi=bi):
                        def f(eng):
                            S.emit_stream(ek, eng, sems, bi, junk=junkc)
                        return f
                    regs[ek](make())
        with nc.Block() as blk:
            @blk.gpsimd
            def _(g):
                for k, v in sorted(S.counts.items()):
                    if k in sems and v and k not in ("T", "V", "S", "G"):
                        g.wait_ge(sems[k], v)
    return nc


def kernel(L0_attn_x, L0_attn_Wq, L0_attn_Wk, L0_attn_Wv, L0_attn_Wg, L0_attn_bg,
           L0_attn_Wo, L0_ffn_w1, L0_ffn_conv_w, L0_ffn_conv_b, L0_ffn_w2,
           _debug=False):
    x = np.ascontiguousarray(np.asarray(L0_attn_x, np.float32))
    nc = build_program(debug=_debug)
    base = {
        "wq": np.ascontiguousarray(L0_attn_Wq, np.float32),
        "wk": np.ascontiguousarray(L0_attn_Wk, np.float32),
        "wv": np.ascontiguousarray(L0_attn_Wv, np.float32),
        "wg": np.ascontiguousarray(L0_attn_Wg, np.float32),
        "bg": np.ascontiguousarray(np.asarray(L0_attn_bg, np.float32).reshape(1, D)),
        "wo": np.ascontiguousarray(L0_attn_Wo, np.float32),
        "w1": np.ascontiguousarray(L0_ffn_w1, np.float32),
        "w2": np.ascontiguousarray(L0_ffn_w2, np.float32),
        "cwT": np.ascontiguousarray(np.asarray(L0_ffn_conv_w, np.float32)[:, 0, :].T),
        "cb": np.ascontiguousarray(np.asarray(L0_ffn_conv_b, np.float32).reshape(1, DH4)),
    }
    for name, arr in _CONST.items():
        base[name] = arr
    in_maps = []
    for c in range(8):
        m = dict(base)
        m["x"] = np.ascontiguousarray(x[2 * c:2 * c + 2])
        in_maps.append(m)
    res = run_bass_kernel_spmd(nc, in_maps, core_ids=list(range(8)))
    out = np.concatenate([np.asarray(r["out"], np.float32) for r in res.results], axis=0)
    if _debug:
        kernel._dbg = res.results
    return out.reshape(16, SEQLEN, D)




# version 13 (final)
# speedup vs baseline: 1.1337x; 1.1337x over previous
import numpy as np
import ml_dtypes
import concourse.bass as bass
import concourse.mybir as mybir
from concourse.bass_utils import run_bass_kernel_spmd

I16 = mybir.dt.int16
F16 = mybir.dt.float16
F32 = mybir.dt.float32
F32R = mybir.dt.float32r
BF16 = mybir.dt.bfloat16
I32 = mybir.dt.int32
AF = mybir.ActivationFunctionType
ALU = mybir.AluOpType
AX = mybir.AxisListType
BF = ml_dtypes.bfloat16

D = 1024
H = 16
DH = 64
DH4 = 4096
T = 128
NSEQ = 2
SEQLEN = 2048
NT = SEQLEN // T


def _host_consts():
    c = {}
    c["idn16"] = np.eye(128, dtype=np.float32).astype(BF)
    tri = np.zeros((128, 128), np.float32)
    for blk in range(4):
        for j in range(32):
            tri[blk * 32 + j, blk * 32 + j:blk * 32 + 32] = -1.0
    c["tri32"] = tri
    m = np.zeros((128, 512), np.float32)
    for r in range(128):
        k = r % 64
        for hh in range(8):
            for q in range(64):
                if k // 32 == q // 32 and k % 32 <= q % 32:
                    m[r, hh * 64 + q] = 1.0
    c["maskp"] = m.astype(BF)
    rs = np.zeros((128, 8, 128), np.float32)
    for cc in range(4):
        rs[32 * cc + 31, 2 * cc, 0:64] = 1.0
        rs[32 * cc + 31, 2 * cc + 1, 64:128] = 1.0
    c["rowsel16"] = rs.astype(BF)
    c["ones_col16"] = np.ones((128, 1), np.float32).astype(BF)
    c["ones_row16"] = np.ones((1, 128), np.float32).astype(BF)
    c["ones_row32"] = np.ones((1, 128), np.float32)
    c["ones_col32"] = np.ones((128, 1), np.float32)
    shm = np.zeros((128, 128), np.float32)
    for t in range(1, 128):
        shm[t - 1, t] = 1.0
    shp = np.zeros((128, 128), np.float32)
    for t in range(127):
        shp[t + 1, t] = 1.0
    e31 = np.zeros((1, 32), np.float32); e31[0, 31] = 1.0
    sh1 = np.zeros((128, 128), np.float32)
    for t in range(1, 128):
        sh1[t - 1, t] = 1.0
    sh2 = np.zeros((128, 128), np.float32)
    for t in range(2, 128):
        sh2[t - 2, t] = 1.0
    c["sh1_r"] = sh1
    c["sh2_r"] = sh2
    c["idn_r"] = np.eye(128, dtype=np.float32)
    er = np.zeros((4, 128), np.float32)
    er[0, 0] = 1.0
    er[1, 0] = 1.0
    er[2, 1] = 1.0
    er[3, :] = 1.0
    c["erows_h"] = er
    return c


_CONST = _host_consts()


class Op:
    __slots__ = ("block", "eng", "fn", "waits", "mark", "dmas", "dsem", "drains")
    def __init__(self, block, eng, fn, waits=(), mark=None, dmas=0, dsem=None,
                 drains=()):
        self.block, self.eng, self.fn = block, eng, fn
        self.waits, self.mark, self.dmas = [w for w in waits if w], mark, dmas
        self.dsem = dsem
        self.drains = [d for d in drains if d]


class SpacedEng:
    _ALU = {"tensor_tensor", "tensor_scalar", "tensor_copy",
            "scalar_tensor_tensor", "tensor_reduce", "reduce_max", "memset",
            "reciprocal", "activation", "copy", "iota"}

    def __init__(self, eng, junk):
        self._eng, self._junk = eng, junk
        self._narrow = False

    @staticmethod
    def _width(a):
        try:
            return int(np.prod([d[1] for d in a.ap[1:]]))
        except Exception:
            return 1 << 30

    def __getattr__(self, nm):
        f = getattr(self._eng, nm)
        if nm not in SpacedEng._ALU or not callable(f):
            return f

        def wrap(*args, **kw):
            if self._narrow:
                self._eng.tensor_scalar(self._junk[:, :], self._junk[:, :],
                                        1.0, None, ALU.mult)
            r = f(*args, **kw)
            out = args[0] if args else None
            self._narrow = (out is not None and hasattr(out, "ap")
                            and SpacedEng._width(out) < 64)
            return r
        return wrap


class Sched:
    def __init__(self, start=None):
        self.ops = []
        self.marks = {}
        self.counts = dict(start) if start else {k: 0 for k in ("Y", "T", "V", "S", "G", "GD")}
    def add(self, block, eng, fn, waits=(), mark=None, dmas=0, dsem=None,
            drains=()):
        assert (dmas == 0) == (dsem is None), f"dma op needs dsem: {mark}"
        self.ops.append(Op(block, eng, fn, waits, mark, dmas, dsem, drains))
    def plan(self):
        for op in self.ops:
            if op.dmas:
                key = op.dsem
                self.counts[key] = self.counts.get(key, 0) + 16 * op.dmas
                if op.mark:
                    self.marks[op.mark] = (key, self.counts[key])
            elif op.mark:
                self.counts[op.eng] += 1
                self.marks[op.mark] = (op.eng, self.counts[op.eng])
        missing = [w for op in self.ops for w in op.waits if w not in self.marks]
        assert not missing, f"unresolved waits: {sorted(set(missing))[:10]}"
    def emit_stream(self, eng_key, eng, sems, block_idx, junk=None):
        last_wait = {}
        spaced = SpacedEng(eng, junk) if junk is not None and \
            eng_key in ("V", "G") else None
        for op in self.ops:
            if op.block != block_idx or op.eng != eng_key:
                continue
            for w in op.waits:
                sk, val = self.marks[w]
                if sk == eng_key:
                    continue
                if last_wait.get(sk, -1) >= val:
                    continue
                eng.wait_ge(sems[sk], val)
                last_wait[sk] = val
            for d in op.drains:
                sk, val = self.marks[d]
                eng.wait_ge(sems[sk], val)
            if op.dmas:
                def dma(out, in_, _k=op.dsem, _e=eng, **kw):
                    return _e.dma_start(out, in_, **kw).then_inc(sems[_k], 16)
                op.fn(eng, dma)
            else:
                inst = op.fn(spaced if spaced is not None else eng)
                if op.mark:
                    assert inst is not None, f"mark {op.mark} with no instruction"
                    inst.then_inc(sems[op.eng], 1)


def build_program(n_tiles=NT, debug=False, max_block=4):
    nc = bass.Bass("TRN2", target_bir_lowering=False, detect_race_conditions=False)
    S = Sched()

    d_x = nc.dram_tensor("x", [NSEQ, SEQLEN, D], F32, kind="ExternalInput")
    d_w = {n: nc.dram_tensor(n, [D, D], F32, kind="ExternalInput")
           for n in ("wq", "wk", "wv", "wg", "wo")}
    d_bg = nc.dram_tensor("bg", [1, D], F32, kind="ExternalInput")
    d_w1 = nc.dram_tensor("w1", [DH4, D], F32, kind="ExternalInput")
    d_w2 = nc.dram_tensor("w2", [D, DH4], F32, kind="ExternalInput")
    d_cwT = nc.dram_tensor("cwT", [3, DH4], F32, kind="ExternalInput")
    d_cb = nc.dram_tensor("cb", [1, DH4], F32, kind="ExternalInput")
    dc = {}
    for name, arr in _CONST.items():
        dt = BF16 if arr.dtype == BF else (F16 if arr.dtype == np.float16 else F32)
        dc[name] = nc.dram_tensor(name, list(arr.shape), dt, kind="ExternalInput")
    d_out = nc.dram_tensor("out", [NSEQ, SEQLEN, D], F32, kind="ExternalOutput")
    _ik = "ExternalOutput" if debug else "Internal"
    d_s_scr = nc.dram_tensor("s_scr", [NSEQ, SEQLEN, D], BF16, kind=_ik)
    d_xq_scr = nc.dram_tensor("xq_scr", [NSEQ, NT, 128, 8, 128], BF16, kind=_ik)
    d_s1_scr = nc.dram_tensor("s1_scr", [NSEQ, NT, 128, 1], F32, kind=_ik)
    d_w1q_scr = nc.dram_tensor("w1q_scr", [DH4, D], BF16, kind=_ik)
    d_w2q_scr = nc.dram_tensor("w2q_scr", [D, DH4], BF16, kind=_ik)
    d_y2a_scr = nc.dram_tensor("y2a_scr", [NSEQ, NT + 1, 128, 32, 128], BF16, kind=_ik)
    d_s2_scr = nc.dram_tensor("s2_scr", [NSEQ, NT + 1, 128, 1], F32, kind=_ik)
    dbg = {}
    if debug:
        for nm, shp, dt in [("dbg_qk", [128, 2048], BF16), ("dbg_sp", [128, 1024], F32),
                            ("dbg_vt", [128, 1024], BF16), ("dbg_a16", [128, 1024], BF16),
                            ("dbg_s0", [128, 512], BF16), ("dbg_sout", [128, 1024], BF16),
                            ("dbg_xq", [128, 1024], BF16), ("dbg_h1", [128, 4096], BF16),
                            ("dbg_ys", [128, 4096], F32), ("dbg_y2s", [128, 4096], BF16),
                            ("dbg_g", [128, 1024], F32),
                            ("dbg_xb", [128, 1024], F32),
                            ("dbg_x16b", [128, 1024], BF16),
                            ("dbg_cols", [128, 24], F32),
                            ("dbg_x16a", [128, 8, 128], BF16)]:
            dbg[nm] = nc.dram_tensor(nm, shp, dt, kind="ExternalOutput")

    import contextlib
    with contextlib.ExitStack() as ctx:
        alloc = lambda *a: ctx.enter_context(nc.sbuf_tensor(*a))
        palloc = lambda *a: ctx.enter_context(nc.psum_tensor(*a))
        ctxA = contextlib.ExitStack()
        allocA = lambda *a: ctxA.enter_context(nc.sbuf_tensor(*a))
        pallocA = lambda *a: ctxA.enter_context(nc.psum_tensor(*a))
        _A_ONLY = {"tri32", "maskp", "rowsel16"}
        _B_ONLY = {"sh1_r", "sh2_r", "idn_r"}
        cst = {}
        def _dt_of(arr):
            if arr.dtype == BF:
                return BF16
            if arr.dtype == np.float16:
                return F16
            return F32
        for name, arr in _CONST.items():
            if name in _A_ONLY or name in _B_ONLY:
                continue
            cst[name] = alloc(name + "_sb", list(arr.shape), _dt_of(arr))
        swcols = alloc("swcols", [128, 4], F32)
        junkc = alloc("junkc", [128, 64], F32)
        smalls = alloc("smalls", [1, 8], F32)
        class SemPool(dict):
            def __missing__(self, k):
                self[k] = ctx.enter_context(nc.semaphore("sem" + k))
                return self[k]
        sems = SemPool()
        for k in ("T", "V", "S", "G"):
            sems[k]
        tdc = [0]
        def tdrain(eng, last):
            tdc[0] += 1
            last.then_inc(sems["TD"], 1)
            eng.wait_ge(sems["TD"], tdc[0])
        for name in _A_ONLY:
            cst[name] = allocA(name + "_sb", list(_CONST[name].shape), _dt_of(_CONST[name]))
        bgB = allocA("bgB", [128, D], BF16)
        W4 = allocA("W4", [128, 8, 4, 1024], BF16)
        WO = allocA("WO", [128, 8, 1024], BF16)
        corrL = allocA("corrL", [1, 128], BF16)
        corrR = allocA("corrR", [1, 4096], BF16)
        bgrow = allocA("bgrow", [1, D], BF16)
        xB = [allocA(f"xB{i}", [128, D], F32) for i in range(2)]
        colsD = [allocA(f"cols{i}", [128, 24], F32) for i in range(2)]
        mnegD = [allocA(f"mneg{i}", [128, 1], BF16) for i in range(2)]
        x16B = allocA("x16B", [128, D], BF16)
        x16A = allocA("x16A", [128, 8, 128], BF16)
        QKB = allocA("QKB", [128, 2048], BF16)
        QA = allocA("QA", [128, 8, 128], BF16)
        KA = allocA("KA", [128, 8, 128], BF16)
        VT16 = allocA("VT16", [128, D], BF16)
        VT16z = allocA("VT16z", [128, D], BF16)
        QAzO = allocA("QAzO", [128, 8, 128], BF16)
        A16 = allocA("A16", [128, D], BF16)
        AREC = allocA("AREC", [128, D], BF16)
        SCRZ = allocA("SCRZ", [128, D], F32)
        SCRE = allocA("SCRE", [128, D], F32)
        PTM = allocA("PTM", [128, 1024], BF16)
        SST = [[allocA(f"SST{s}_{p}", [128, 512], BF16) for p in range(2)] for s in range(NSEQ)]
        SCR_S = allocA("SCR_S", [128, 512], F32)
        OB16 = allocA("OB16", [128, D], BF16)
        OA16 = allocA("OA16", [128, 8, 128], BF16)
        sB16 = allocA("sB16", [128, D], BF16)
        T0 = allocA("T0", [128, D], F32)
        ICST = allocA("ICST", [128, D], I32)
        XQ1B = allocA("XQ1B", [128, D], BF16)
        XQ1A = allocA("XQ1A", [128, 8, 128], BF16)
        WCH = [allocA(f"WCH{i}", [128, 1024], F32) for i in range(2)]
        WQ16 = allocA("WQ16", [128, 1024], BF16)
        WT0 = WCH[1]
        WIC = allocA("WIC", [128, 1024], I32)
        abscols = allocA("abscols", [128, 80], F32)
        pP = pallocA("pP", [128, 2048], F32)
        pPT = pallocA("pPT", [128, 512], F32)
        pU = pallocA("pU", [128, 512], F32)
        pBAC = pallocA("pBAC", [128, 512], F32)
        pTR = pallocA("pTR", [128, 1024], BF16)

        B0 = 0
        def ld_consts(eng, dma):
            for name in _CONST:
                if name in _B_ONLY:
                    continue
                idx = tuple(slice(None) for _ in _CONST[name].shape)
                dma(cst[name][idx], dc[name][idx])
        S.add(B0, "Y", ld_consts, dmas=len(_CONST) - 3, mark="c_ld", dsem="dcst")

        def ld_weights(eng, dma):
            for wi, n in enumerate(("wq", "wk", "wv", "wg")):
                dma(W4[:, :, wi, :], d_w[n][:, :].rearrange("(k p) n -> p k n", p=128))
            dma(WO[:, :, :], d_w["wo"][:, :].rearrange("(k p) n -> p k n", p=128))
            dma(bgrow[:, :], d_bg[:, :])
        S.add(B0, "G", ld_weights, dmas=6, mark="w_ld", dsem="dwt")

        def init_s(eng):
            eng.memset(junkc[:, :], 0.0)
            for s in range(NSEQ):
                for p in range(2):
                    eng.memset(SST[s][p][:, :], 0.0)
            eng.memset(abscols[:, :], 0.0)
            eng.memset(smalls[:, :], 0.0)
            eng.memset(VT16z[:, :], 0.0)
            eng.memset(QAzO[:, :, :], 0.0)
            eng.memset(colsD[0][:, :], 0.0)
            return eng.memset(colsD[1][:, :], 0.0)
        S.add(B0, "V", init_s, mark="v_sinit")

        def bcasts(eng, dma):
            ap = bass.AP(d_bg[0:1, :].tensor, d_bg[0:1, :].offset,
                         [[0, 128]] + [list(p) for p in d_bg[0:1, :].ap[1:]])
            dma(bgB[:, :], ap)
        S.add(B0, "G", bcasts, waits=["c_ld"], dmas=1, mark="v_bgb", dsem="dbc")
        S.add(B0, "V", lambda eng: eng.tensor_copy(smalls[0:1, 7:8], smalls[0:1, 7:8]),
              waits=["v_bgb"], mark="v_cwb_2_7")
        for rnd in range(2):
            def mm_wsums(eng, rnd=rnd):
                last = None
                for nb in range(4):
                    nbg = rnd * 4 + nb
                    for k in range(8):
                        last = eng.matmul(pP[0:1, nb * 512:(nb + 1) * 512],
                                          cst["ones_col16"][:, :],
                                          W4[:, k, nbg // 2, (nbg % 2) * 512:(nbg % 2) * 512 + 512],
                                          start=(k == 0), stop=(k == 7))
                return last
            S.add(B0, "T", mm_wsums, waits=["w_ld", "c_ld", "v_cwb_2_7", "v_bgb"] + (["v_corrR0"] if rnd else []),
                  mark=f"t_wsums{rnd}")
            S.add(B0, "V", lambda eng, rnd=rnd: eng.tensor_copy(
                corrR[0:1, rnd * 2048:(rnd + 1) * 2048], pP[0:1, 0:2048]),
                waits=[f"t_wsums{rnd}"], mark=f"v_corrR{rnd}")
        S.add(B0, "V", lambda eng: eng.tensor_copy(smalls[0:1, 6:7], smalls[0:1, 6:7]),
              waits=["v_corrR1"], mark="v_corrR")


        B1 = 1
        NITER = 2 * n_tiles
        wpieces = [("w1", i, 0) for i in range(32)] + \
                  [("w2", i, q) for i in range(8) for q in range(4)]
        def wp_src(p):
            wname, ci, q = p
            if wname == "w1":
                return d_w1[ci * 128:(ci + 1) * 128, :], d_w1q_scr[ci * 128:(ci + 1) * 128, :]
            return (d_w2[ci * 128:(ci + 1) * 128, q * 1024:(q + 1) * 1024],
                    d_w2q_scr[ci * 128:(ci + 1) * 128, q * 1024:(q + 1) * 1024])

        n_stat_iters = min(8, NITER)
        stats_per_iter = -(-len(wpieces) // n_stat_iters)
        fin_iter = n_stat_iters - 1 if NITER <= n_stat_iters + 1 else n_stat_iters
        q_start = fin_iter + 1 if fin_iter + 1 < NITER else fin_iter
        n_q_iters = max(1, NITER - q_start)
        quant_per_iter = -(-len(wpieces) // n_q_iters)
        last_q_mark = [None]

        for it in range(NITER):
            t, s = it // 2, it % 2
            ib = it % 2
            P = f"i{it}_"
            Pp = f"i{it - 1}_"
            Ppp = f"i{it - 2}_"

            def w_(name, lag=0):
                if it - lag < 0:
                    return None
                return f"i{it - lag}_{name}"


            def emit_stats(it2):
                ib2 = it2 % 2
                P2 = f"i{it2}_"
                cols = colsD[ib2]
                mneg16 = mnegD[ib2]
                t2, s2 = it2 // 2, it2 % 2
                def w2_(name, lag=0):
                    if it2 - lag < 0:
                        return None
                    return f"i{it2 - lag}_{name}"
                def op_ldx(eng, dma, t2=t2, s2=s2, ib2=ib2):
                    dma(xB[ib2][:, :], d_x[s2, t2 * T:(t2 + 1) * T, :])
                S.add(B1, "Y", op_ldx,
                      waits=[w for w in (w2_("v_q1", 2), w2_("ssumsq", 2)) if w],
                      dmas=1, mark=P2 + "dx", dsem=f"dx{ib2}")
                S.add(B1, "S", lambda eng: eng.activation(
                    SCRE[:, :], xB[ib2][:, :], AF.Identity, bias=0.0, scale=1.0,
                    accum_out=cols[:, 0:1]),
                    waits=[w for w in (P2 + "dx", w2_("v_q1", 2), w2_("g_xq", 2))
                           if w], mark=P2 + "ssum")
                S.add(B1, "S", lambda eng: eng.activation(
                    SCRE[:, :], xB[ib2][:, :], AF.Square, bias=0.0, scale=1.0,
                    accum_out=cols[:, 1:2]), waits=[P2 + "dx"], mark=P2 + "ssumsq")
                def op_vmx(eng):
                    eng.reduce_max(cols[:, 5:6], xB[ib2][:, :], AX.X)
                    return eng.tensor_reduce(cols[:, 6:7], xB[ib2][:, :], AX.X,
                                             ALU.min)
                S.add(B1, "V", op_vmx, waits=[P2 + "dx"], mark=P2 + "vmx")
                def op_stats(eng):
                    eng.tensor_scalar(cols[:, 2:3], cols[:, 0:1], 1.0 / D, None, ALU.mult)
                    eng.tensor_tensor(cols[:, 13:14], cols[:, 2:3], cols[:, 2:3], ALU.mult)
                    eng.tensor_scalar(cols[:, 14:15], cols[:, 1:2], 1.0 / D, None, ALU.mult)
                    eng.tensor_tensor(cols[:, 3:4], cols[:, 14:15], cols[:, 13:14],
                                      ALU.subtract)
                    eng.tensor_scalar(cols[:, 3:4], cols[:, 3:4], 1e-5, None, ALU.add)
                    return eng.tensor_scalar(mneg16[:, :], cols[:, 2:3], -1.0, None,
                                             ALU.mult)
                S.add(B1, "V", op_stats,
                      waits=[w for w in (P2 + "ssum", P2 + "ssumsq",
                                         w2_("t_mrow", 2)) if w],
                      mark=P2 + "vstat")
                S.add(B1, "S", lambda eng: eng.activation(
                    cols[:, 15:16], cols[:, 3:4], AF.Ln, bias=0.0, scale=1.0),
                    waits=[P2 + "vstat"], mark=P2 + "lnv")
                S.add(B1, "V", lambda eng: eng.tensor_scalar(
                    cols[:, 16:17], cols[:, 15:16], -0.5, None, ALU.mult),
                    waits=[P2 + "lnv"], mark=P2 + "nhalf")
                S.add(B1, "S", lambda eng: eng.activation(
                    cols[:, 4:5], cols[:, 16:17], AF.Exp, bias=0.0, scale=1.0),
                    waits=[P2 + "nhalf"], mark=P2 + "r")
                def op_rfix(eng):
                    eng.tensor_tensor(cols[:, 22:23], cols[:, 4:5], cols[:, 4:5], ALU.mult)
                    eng.tensor_tensor(cols[:, 22:23], cols[:, 22:23], cols[:, 3:4], ALU.mult)
                    eng.tensor_scalar(cols[:, 22:23], cols[:, 22:23], -0.5, 1.5,
                                      ALU.mult, ALU.add)
                    return eng.tensor_tensor(cols[:, 4:5], cols[:, 4:5], cols[:, 22:23],
                                             ALU.mult)
                S.add(B1, "V", op_rfix, waits=[P2 + "r"], mark=P2 + "rfix")
                def op_qcols(eng):
                    eng.tensor_tensor(cols[:, 17:18], cols[:, 5:6], cols[:, 2:3], ALU.subtract)
                    eng.tensor_tensor(cols[:, 18:19], cols[:, 2:3], cols[:, 6:7], ALU.subtract)
                    eng.tensor_tensor(cols[:, 18:19], cols[:, 17:18], cols[:, 18:19], ALU.max)
                    eng.tensor_tensor(cols[:, 19:20], cols[:, 18:19], cols[:, 4:5], ALU.mult)
                    eng.tensor_scalar(cols[:, 7:8], cols[:, 19:20], 1e-5, None, ALU.max)
                    eng.reciprocal(cols[:, 20:21], cols[:, 7:8])
                    eng.tensor_tensor(cols[:, 23:24], cols[:, 7:8], cols[:, 20:21], ALU.mult)
                    eng.tensor_scalar(cols[:, 23:24], cols[:, 23:24], -1.0, 2.0,
                                      ALU.mult, ALU.add)
                    eng.tensor_tensor(cols[:, 20:21], cols[:, 20:21], cols[:, 23:24], ALU.mult)
                    eng.tensor_scalar(cols[:, 8:9], cols[:, 20:21], 127.0, None, ALU.mult)
                    eng.tensor_scalar(cols[:, 9:10], cols[:, 7:8], 1.0 / 127.0, None, ALU.mult)
                    eng.tensor_tensor(cols[:, 10:11], cols[:, 4:5], cols[:, 8:9], ALU.mult)
                    eng.tensor_tensor(cols[:, 21:22], cols[:, 2:3], cols[:, 10:11], ALU.mult)
                    return eng.tensor_scalar(cols[:, 11:12], cols[:, 21:22], -1.0, None,
                                             ALU.mult)
                S.add(B1, "V", op_qcols,
                      waits=[w for w in (P2 + "rfix", P2 + "vmx",
                                         w2_("g_xq", 2)) if w],
                      mark=P2 + "qcols")

            cols = colsD[ib]
            mneg16 = mnegD[ib]
            if it == 0:
                emit_stats(0)
            S.add(B1, "V", lambda eng, ib=ib: eng.tensor_copy(
                x16B[:, :], xB[ib][:, :]),
                waits=[P + "dx", w_("t_x16t", 1)], mark=P + "vmm")

            def op_trm(eng, mneg16=mneg16):
                return eng.transpose(pTR[0:1, 0:128], mneg16[:, 0:1], cst["idn16"][:, :])
            S.add(B1, "T", op_trm, waits=[P + "vstat", w_("s_xqa", 1) or "v_sinit"],
                  mark=P + "t_mrow")
            S.add(B1, "S", lambda eng: eng.copy(corrL[0:1, 0:128], pTR[0:1, 0:128]),
                  waits=[P + "t_mrow", w_("t_vg", 1)], mark=P + "corrl")

            def op_x16t(eng):
                last = None
                for k in range(8):
                    last = eng.transpose(pTR[:, k * 128:(k + 1) * 128],
                                         x16B[:, k * 128:(k + 1) * 128], cst["idn16"][:, :])
                return last
            S.add(B1, "T", op_x16t, waits=[P + "vmm", P + "corrl"], mark=P + "t_x16t")
            S.add(B1, "S", lambda eng: eng.copy(x16A[:, :, :], pTR[:, 0:1024]),
                  waits=[P + "t_x16t"], mark=P + "s_x16a")
            if debug and it == 0:
                S.add(B1, "G", lambda eng, dma: dma(
                    dbg["dbg_x16a"][:, :, :], x16A[:, :, :]),
                    waits=[P + "s_x16a"], dmas=1, dsem="ddbg")

            def op_qk(eng):
                last = None
                for k in range(8):
                    for nb in range(4):
                        last = eng.matmul(pP[:, nb * 512:(nb + 1) * 512],
                                          x16A[:, k, :],
                                          W4[:, k, nb // 2, (nb % 2) * 512:(nb % 2) * 512 + 512],
                                          start=(k == 0), stop=False)
                for nb in range(4):
                    last = eng.matmul(pP[:, nb * 512:(nb + 1) * 512], corrL[0:1, :],
                                      corrR[0:1, nb * 512:(nb + 1) * 512],
                                      start=False, stop=True)
                return last
            S.add(B1, "T", op_qk,
                  waits=[P + "s_x16a", "v_corrR", w_("v_sevac", 1), w_("v_oevac", 1)],
                  mark=P + "t_qk")

            S.add(B1, "S", lambda eng, cols=cols: eng.activation(
                QKB[:, :], pP[:, :], AF.Identity, bias=0.0, scale=cols[:, 4:5]),
                waits=[P + "t_qk", P + "rfix", w_("t_u3", 1)], mark=P + "v_qkev")
            if debug and it == 0:
                S.add(B1, "G", lambda eng, dma: dma(dbg["dbg_qk"][:, :], QKB[:, :]),
                      waits=[P + "v_qkev"], dmas=1, dsem="ddbg")

            def op_vg(eng):
                last = None
                for k in range(8):
                    for nb in range(4):
                        last = eng.matmul(pP[:, nb * 512:(nb + 1) * 512],
                                          x16A[:, k, :],
                                          W4[:, k, 2 + nb // 2, (nb % 2) * 512:(nb % 2) * 512 + 512],
                                          start=(k == 0), stop=False)
                for nb in range(4):
                    last = eng.matmul(pP[:, nb * 512:(nb + 1) * 512], corrL[0:1, :],
                                      corrR[0:1, 2048 + nb * 512:2048 + (nb + 1) * 512],
                                      start=False, stop=True)
                return last
            S.add(B1, "T", op_vg, waits=[P + "v_qkev"], mark=P + "t_vg")

            def op_qt(eng):
                last = None
                for k in range(8):
                    last = eng.transpose(pTR[:, k * 128:(k + 1) * 128],
                                         QKB[:, k * 128:(k + 1) * 128], cst["idn16"][:, :])
                return last
            S.add(B1, "T", op_qt, waits=[P + "v_qkev", P + "s_x16a"], mark=P + "t_qt")
            S.add(B1, "S", lambda eng: eng.copy(QA[:, :, :], pTR[:, 0:1024]),
                  waits=[P + "t_qt", w_("t_oeA1", 1), w_("v_qaz", 1)],
                  mark=P + "s_qa")
            def op_kt(eng):
                last = None
                for k in range(8):
                    last = eng.transpose(pTR[:, k * 128:(k + 1) * 128],
                                         QKB[:, 1024 + k * 128:1024 + (k + 1) * 128],
                                         cst["idn16"][:, :])
                return last
            S.add(B1, "T", op_kt, waits=[P + "s_qa"], mark=P + "t_kt")
            S.add(B1, "S", lambda eng: eng.copy(KA[:, :, :], pTR[:, 0:1024]),
                  waits=[P + "t_kt", w_("t_p11", 1)], mark=P + "s_ka")

            S.add(B1, "V", lambda eng, cols=cols: eng.scalar_tensor_tensor(
                SCRZ[:, :], pP[:, 1024:2048], cols[:, 4:5], bgB[:, :], ALU.mult, ALU.add),
                waits=[P + "t_vg", "v_bgb", w_("t_cums", 1)], mark=P + "v_zev")
            S.add(B1, "S", lambda eng: eng.activation(
                SCRE[:, :], SCRZ[:, :], AF.Exp, bias=0.0, scale=-1.0),
                waits=[P + "v_zev"], mark=P + "s_spe")
            S.add(B1, "S", lambda eng: eng.activation(
                SCRZ[:, :], SCRE[:, :], AF.Ln, bias=1.0, scale=1.0),
                waits=[P + "s_spe"], mark=P + "s_sp")
            if debug and it == 0:
                S.add(B1, "G", lambda eng, dma: dma(dbg["dbg_sp"][:, :], SCRZ[:, :]),
                      waits=[P + "s_sp"], dmas=1, dsem="ddbg")

            for pi_ in range(2):
                pr_ = 64 * pi_
                for g in range(2):
                    def op_pg(eng, pr=pr_, g=g):
                        last = None
                        for hh in (0, 2, 4, 6, None, 1, 3, 5, 7):
                            if hh is None:
                                tdrain(eng, last)
                                continue
                            h = g * 8 + hh
                            pb = 64 * (h % 2)
                            last = eng.matmul(pPT[pr:pr + 64, hh * 64:(hh + 1) * 64],
                                              KA[pb:pb + 64, h // 2, pr:pr + 64],
                                              QA[pb:pb + 64, h // 2, pr:pr + 64],
                                              start=True, stop=True)
                        return last
                    wp = [P + "s_ka"]
                    if g:
                        wp.append(P + f"v_ptm{pi_}0")
                    else:
                        wp.append((P + f"v_ptm{pi_ - 1}1") if pi_
                                  else w_("v_ptm11", 1))
                    if it == q_start and pi_ == 0 and g == 0 and q_start > fin_iter:
                        wp.append("v_swcols")
                    S.add(B1, "T", op_pg, waits=[w for w in wp if w],
                          mark=P + f"t_p{pi_}{g}")
                    S.add(B1, "V", lambda eng, pr=pr_, g=g: eng.tensor_tensor(
                        PTM[pr:pr + 64, g * 512:(g + 1) * 512],
                        pPT[pr:pr + 64, 0:512],
                        cst["maskp"][pr:pr + 64, :], ALU.mult),
                        waits=[P + f"t_p{pi_}{g}"] +
                              ([w_(f"t_oi{pi_}", 1)] if w_(f"t_oi{pi_}", 1) else []),
                        mark=P + f"v_ptm{pi_}{g}")

            def op_cums(eng):
                eng.matmul(pP[:, 1024:1536], cst["tri32"][:, :], SCRZ[:, 0:512],
                           start=True, stop=True)
                return eng.matmul(pP[:, 1536:2048], cst["tri32"][:, :], SCRZ[:, 512:1024],
                                  start=True, stop=True)
            S.add(B1, "T", op_cums, waits=[P + "s_sp", P + "v_zev"], mark=P + "t_cums")

            S.add(B1, "S", lambda eng: eng.activation(
                A16[:, :], pP[:, 1024:2048], AF.Exp, bias=0.0, scale=1.0),
                waits=[P + "t_cums", w_("v_oevac", 1), w_("t_bac3", 1)], mark=P + "s_a16")
            S.add(B1, "S", lambda eng: eng.activation(
                AREC[:, :], pP[:, 1024:2048], AF.Exp, bias=0.0, scale=-1.0),
                waits=[P + "t_cums", w_("v_vt", 1)], mark=P + "s_arec")
            if debug and it == 0:
                S.add(B1, "G", lambda eng, dma: dma(dbg["dbg_a16"][:, :], A16[:, :]),
                      waits=[P + "s_a16"], dmas=1, dsem="ddbg")

            S.add(B1, "V", lambda eng, cols=cols: eng.scalar_tensor_tensor(
                VT16[:, :], pP[:, 0:1024], cols[:, 4:5], AREC[:, :], ALU.mult, ALU.mult),
                waits=[P + "s_arec", P + "t_vg", w_("t_u3", 1)], mark=P + "v_vt")
            if debug and it == 0:
                S.add(B1, "G", lambda eng, dma: dma(dbg["dbg_vt"][:, :], VT16[:, :]),
                      waits=[P + "v_vt"], dmas=1, dsem="ddbg")

            S.add(B1, "V", lambda eng: (
                eng.tensor_copy(QAzO[:, :, 32:64], QA[:, :, 32:64]),
                eng.tensor_copy(QAzO[:, :, 96:128], QA[:, :, 96:128]))[-1],
                waits=[P + "s_qa", w_("t_oeB1", 1)], mark=P + "v_qaz")
            S.add(B1, "V", lambda eng: eng.tensor_copy(
                VT16z[96:128, :], VT16[96:128, :]),
                waits=[P + "v_vt", w_("t_u3", 1)], mark=P + "v_vtz")

            if it + 1 < NITER:
                emit_stats(it + 1)

            sel = SST[s]

            def add_bac(cc):
                def op_bac(eng, cc=cc):
                    av = A16[:, :].rearrange("p (a b) -> p a b", a=8)
                    eng.matmul(pBAC[:, :], cst["rowsel16"][:, 2 * cc, :], av[:, :, 0:64],
                               start=True, stop=False)
                    return eng.matmul(pBAC[:, :], cst["rowsel16"][:, 2 * cc + 1, :],
                                      av[:, :, 64:128], start=False, stop=True)
                wb = [P + "s_a16",
                      (P + f"v_supd{cc - 1}") if cc else w_("v_supd3", 1)]
                S.add(B1, "T", op_bac, waits=[w for w in wb if w],
                      mark=P + f"t_bac{cc}")

            for pi in range(2):
                pr = 64 * pi
                ccA, ccB = 2 * pi, 2 * pi + 1
                S_A, S_B = sel[0], sel[1]

                def op_oi(eng, pr=pr):
                    last = None
                    for h in range(16):
                        g, hh = divmod(h, 8)
                        last = eng.matmul(
                            pP[pr:pr + 64, h * 64:(h + 1) * 64],
                            PTM[pr:pr + 64, g * 512 + hh * 64:g * 512 + (hh + 1) * 64],
                            VT16[pr:pr + 64, h * 64:(h + 1) * 64],
                            start=(h in (0, 8)), stop=False)
                    return last
                woi = [P + f"v_ptm{pi}1", P + "v_vt"]
                if w_("v_oevac", 1):
                    woi.append(w_("v_oevac", 1))
                S.add(B1, "T", op_oi, waits=woi, mark=P + f"t_oi{pi}")

                def op_oeA(eng, ccA=ccA, S_A=S_A):
                    last = None
                    for h in (0, 2, 4, 6, 8, 10, 12, 14, None,
                              1, 3, 5, 7, 9, 11, 13, 15):
                        if h is None:
                            tdrain(eng, last)
                            continue
                        pb = 64 * (h % 2)
                        last = eng.matmul(
                            pP[ccA * 32:ccA * 32 + 32, h * 64:(h + 1) * 64],
                            QA[pb:pb + 64, h // 2, ccA * 32:ccA * 32 + 32],
                            S_A[pb:pb + 64, 64 * (h // 2):64 * (h // 2) + 64],
                            start=False, stop=False)
                    return last
                woa = [P + f"t_oi{pi}"]
                if pi:
                    woa.append(P + "v_supd1")
                elif w_("v_supd3", 1):
                    woa.append(w_("v_supd3", 1))
                S.add(B1, "T", op_oeA, waits=woa, mark=P + f"t_oeA{pi}")

                add_bac(ccA)
                def op_uA(eng, ccA=ccA):
                    last = None
                    for h in range(16):
                        pb = 64 * (h % 2)
                        last = eng.matmul(
                            pU[pb:pb + 64, 64 * (h // 2):64 * (h // 2) + 64],
                            QKB[ccA * 32:(ccA + 1) * 32, 1024 + h * 64:1024 + (h + 1) * 64],
                            VT16[ccA * 32:(ccA + 1) * 32, h * 64:(h + 1) * 64],
                            start=True, stop=True)
                    return last
                wuA = [P + "v_vt",
                       (P + "v_supd1") if pi else w_("v_supd3", 1)]
                S.add(B1, "T", op_uA, waits=[w for w in wuA if w],
                      mark=P + f"t_u{ccA}",
                      drains=[P + f"t_bac{ccA}"] if ccA == 2 else ())
                S.add(B1, "V", lambda eng, S_A=S_A, S_B=S_B: (
                    eng.tensor_tensor(SCR_S[:, :], S_A[:, :], pU[:, :], ALU.add),
                    eng.tensor_tensor(S_B[:, :], SCR_S[:, :], pBAC[:, :], ALU.mult))[-1],
                    waits=[P + f"t_u{ccA}", P + f"t_bac{ccA}", P + f"t_oeA{pi}"],
                    mark=P + f"v_supd{ccA}")

                add_bac(ccB)
                def op_oeB(eng, pr=pr, S_B=S_B):
                    last = None
                    for h in (0, 2, 4, 6, 8, 10, 12, 14, None,
                              1, 3, 5, 7, 9, 11, 13, 15):
                        if h is None:
                            tdrain(eng, last)
                            continue
                        pb = 64 * (h % 2)
                        last = eng.matmul(
                            pP[pr:pr + 64, h * 64:(h + 1) * 64],
                            QAzO[pb:pb + 64, h // 2, pr:pr + 64],
                            S_B[pb:pb + 64, 64 * (h // 2):64 * (h // 2) + 64],
                            start=False, stop=(h in (7, 15)))
                    return last
                S.add(B1, "T", op_oeB,
                      waits=[P + f"v_supd{ccA}", P + "v_qaz"],
                      mark=P + f"t_oeB{pi}")

                def op_uB(eng, ccB=ccB):
                    last = None
                    for h in range(16):
                        pb = 64 * (h % 2)
                        if ccB == 3:
                            last = eng.matmul(
                                pU[pb:pb + 64, 64 * (h // 2):64 * (h // 2) + 64],
                                QKB[64:128, 1024 + h * 64:1024 + (h + 1) * 64],
                                VT16z[64:128, h * 64:(h + 1) * 64],
                                start=True, stop=True)
                        else:
                            last = eng.matmul(
                                pU[pb:pb + 64, 64 * (h // 2):64 * (h // 2) + 64],
                                QKB[ccB * 32:(ccB + 1) * 32,
                                    1024 + h * 64:1024 + (h + 1) * 64],
                                VT16[ccB * 32:(ccB + 1) * 32, h * 64:(h + 1) * 64],
                                start=True, stop=True)
                    return last
                wuB = [P + "v_vt", P + f"v_supd{ccA}"]
                if ccB == 3:
                    wuB.append(P + "v_vtz")
                S.add(B1, "T", op_uB, waits=wuB, mark=P + f"t_u{ccB}",
                      drains=[P + f"t_oeB{pi}"] if ccB == 1 else ())
                S.add(B1, "V", lambda eng, S_A=S_A, S_B=S_B: (
                    eng.tensor_tensor(SCR_S[:, :], S_B[:, :], pU[:, :], ALU.add),
                    eng.tensor_tensor(S_A[:, :], SCR_S[:, :], pBAC[:, :], ALU.mult))[-1],
                    waits=[P + f"t_u{ccB}", P + f"t_bac{ccB}", P + f"t_oeB{pi}"],
                    mark=P + f"v_supd{ccB}")

            if debug and it <= 1:
                S.add(B1, "G", lambda eng, dma, s=s, t=t: dma(
                    dbg["dbg_s0"][:, :], SST[s][(t * 4 + 4) % 2][:, :]),
                    waits=[P + "v_supd3"], dmas=1, dsem="ddbg")

            S.add(B1, "V", lambda eng: eng.tensor_tensor(
                OB16[:, :], pP[:, 0:1024], A16[:, :], ALU.mult),
                waits=[P + "t_oeB1", P + "s_a16", w_("t_ot", 1)], mark=P + "v_oevac")

            def op_ot(eng):
                last = None
                for k in range(8):
                    last = eng.transpose(pTR[:, k * 128:(k + 1) * 128],
                                         OB16[:, k * 128:(k + 1) * 128], cst["idn16"][:, :])
                return last
            S.add(B1, "T", op_ot, waits=[P + "v_oevac", P + "s_ka"], mark=P + "t_ot")
            S.add(B1, "S", lambda eng: eng.copy(OA16[:, :, :], pTR[:, 0:1024]),
                  waits=[P + "t_ot"], mark=P + "s_oa")

            def op_wo(eng):
                last = None
                for k in range(8):
                    for nb in range(2):
                        last = eng.matmul(pP[:, 1024 + nb * 512:1024 + (nb + 1) * 512],
                                          OA16[:, k, :], WO[:, k, nb * 512:(nb + 1) * 512],
                                          start=(k == 0), stop=(k == 7))
                return last
            S.add(B1, "T", op_wo, waits=[P + "s_oa", P + "s_arec", P + "s_a16"],
                  mark=P + "t_wo")
            S.add(B1, "S", lambda eng: eng.copy(sB16[:, :], pP[:, 1024:2048]),
                  waits=[P + "t_wo", w_("g_s", 1)], mark=P + "v_sevac")
            S.add(B1, "G", lambda eng, dma, t=t, s=s: dma(
                d_s_scr[s, t * T:(t + 1) * T, :], sB16[:, :]),
                waits=[P + "v_sevac"], dmas=1, mark=P + "g_s", dsem="dsts")
            if debug and it == 0:
                S.add(B1, "G", lambda eng, dma: dma(dbg["dbg_sout"][:, :], sB16[:, :]),
                      waits=[P + "v_sevac"], dmas=1, dsem="ddbg")

            def op_q1(eng, ib=ib, cols=cols):
                return eng.tensor_scalar(T0[:, :], xB[ib][:, :], cols[:, 10:11],
                                         cols[:, 11:12], ALU.mult, ALU.add)
            S.add(B1, "V", op_q1, waits=[P + "qcols", w_("g_q2", 1)], mark=P + "v_q1")
            S.add(B1, "S", lambda eng: eng.activation(
                XQ1B[:, :], T0[:, :], AF.Sign, bias=0.0, scale=1.0),
                waits=[P + "v_q1", w_("t_xqt", 1)], mark=P + "s_sgn1")
            def op_q2(eng):
                eng.tensor_tensor(T0[:, :], T0[:, :], XQ1B[:, :], ALU.mult)
                return eng.tensor_copy(ICST[:, :], T0[:, :])
            S.add(B1, "V", op_q2, waits=[P + "s_sgn1"], mark=P + "g_q2")
            def op_q3(eng):
                eng.tensor_tensor(T0[:, :], T0[:, :], ICST[:, :], ALU.subtract)
                eng.tensor_scalar(T0[:, :], T0[:, :], 0.5, None, ALU.is_ge)
                eng.tensor_tensor(T0[:, :], T0[:, :], ICST[:, :], ALU.add)
                eng.tensor_scalar(T0[:, :], T0[:, :], 127.0, None, ALU.min)
                return eng.tensor_tensor(XQ1B[:, :], XQ1B[:, :], T0[:, :], ALU.mult)
            S.add(B1, "V", op_q3, waits=[P + "g_q2"], mark=P + "v_q3")
            if debug and it == 0:
                S.add(B1, "G", lambda eng, dma: dma(dbg["dbg_xq"][:, :], XQ1B[:, :]),
                      waits=[P + "v_q3"], dmas=1, dsem="ddbg")

            def op_xqt(eng):
                last = None
                for k in range(8):
                    last = eng.transpose(pTR[:, k * 128:(k + 1) * 128],
                                         XQ1B[:, k * 128:(k + 1) * 128], cst["idn16"][:, :])
                return last
            S.add(B1, "T", op_xqt, waits=[P + "v_q3", P + "s_oa"], mark=P + "t_xqt")
            S.add(B1, "S", lambda eng: eng.copy(XQ1A[:, :, :], pTR[:, 0:1024]),
                  waits=[P + "t_xqt", w_("g_xq", 1)], mark=P + "s_xqa")
            S.add(B1, "G", lambda eng, dma, t=t, s=s, cols=cols: (
                dma(d_xq_scr[s, t, :, :, :], XQ1A[:, :, :]),
                dma(d_s1_scr[s, t, :, :], cols[:, 9:10])),
                waits=[P + "s_xqa", P + "qcols"], dmas=2, mark=P + "g_xq", dsem="dstxq")

            if it < n_stat_iters:
                for k in range(stats_per_iter):
                    ci = it * stats_per_iter + k
                    if ci >= len(wpieces):
                        continue
                    src, _ = wp_src(wpieces[ci])
                    wb = ci % 2
                    S.add(B1, "Y", lambda eng, dma, src=src, wb=wb: dma(WCH[wb][:, :], src),
                          waits=[f"wstat_{ci - 2}"] if ci >= 2 else ["c_ld"],
                          dmas=1, mark=f"wld_{ci}", dsem=f"dwl{ci % 2}")
                    S.add(B1, "S", lambda eng, wb=wb, ci=ci: eng.activation(
                        SCRE[:, :], WCH[wb][:, :], AF.Abs, bias=0.0, scale=1.0,
                        accum_out=abscols[:, ci % 64:ci % 64 + 1]),
                        waits=[f"wld_{ci}", "v_sinit"], mark=f"wstat_{ci}")

            if it == fin_iter:
                def op_wmean(eng):
                    eng.tensor_reduce(abscols[:, 72:73], abscols[:, 0:32], AX.X, ALU.add)
                    return eng.tensor_reduce(abscols[:, 73:74], abscols[:, 32:64], AX.X, ALU.add)
                S.add(B1, "V", op_wmean, waits=[f"wstat_{len(wpieces) - 1}"], mark="v_wmean")
                def op_wtot(eng):
                    eng.matmul(pPT[0:1, 256:257], cst["ones_col32"][:, :], abscols[:, 72:73],
                               start=True, stop=True)
                    return eng.matmul(pPT[0:1, 257:258], cst["ones_col32"][:, :],
                                      abscols[:, 73:74], start=True, stop=True)
                S.add(B1, "T", op_wtot, waits=["v_wmean", P + "v_ptm11"], mark="t_wtot")
                def op_wsc(eng):
                    eng.tensor_scalar(smalls[0:1, 0:1], pPT[0:1, 256:257],
                                      1.0 / (DH4 * D), None, ALU.mult)
                    eng.tensor_scalar(smalls[0:1, 1:2], pPT[0:1, 257:258],
                                      1.0 / (DH4 * D), None, ALU.mult)
                    eng.tensor_scalar(smalls[0:1, 2:3], smalls[0:1, 0:1], 1e-5, None, ALU.max)
                    eng.tensor_scalar(smalls[0:1, 3:4], smalls[0:1, 1:2], 1e-5, None, ALU.max)
                    eng.reciprocal(smalls[0:1, 4:5], smalls[0:1, 2:3])
                    eng.reciprocal(smalls[0:1, 5:6], smalls[0:1, 3:4])
                    for rc, dc_ in ((4, 2), (5, 3)):
                        eng.tensor_tensor(smalls[0:1, 6:7], smalls[0:1, dc_:dc_ + 1],
                                          smalls[0:1, rc:rc + 1], ALU.mult)
                        eng.tensor_scalar(smalls[0:1, 6:7], smalls[0:1, 6:7], -1.0, 2.0,
                                          ALU.mult, ALU.add)
                        eng.tensor_tensor(smalls[0:1, rc:rc + 1], smalls[0:1, rc:rc + 1],
                                          smalls[0:1, 6:7], ALU.mult)
                    return eng.tensor_copy(smalls[0:1, 7:8], smalls[0:1, 5:6])
                S.add(B1, "V", op_wsc, waits=["t_wtot"], mark="v_wsc")
                def op_wscb(eng):
                    eng.matmul(pPT[:, 256:257], cst["ones_row32"][:, :], smalls[0:1, 4:5],
                               start=True, stop=True)
                    eng.matmul(pPT[:, 257:258], cst["ones_row32"][:, :], smalls[0:1, 2:3],
                               start=True, stop=True)
                    eng.matmul(pPT[:, 258:259], cst["ones_row32"][:, :], smalls[0:1, 5:6],
                               start=True, stop=True)
                    return eng.matmul(pPT[:, 259:260], cst["ones_row32"][:, :],
                                      smalls[0:1, 3:4], start=True, stop=True)
                S.add(B1, "T", op_wscb, waits=["v_wsc"], mark="t_wscb")
                S.add(B1, "V", lambda eng: eng.tensor_copy(swcols[:, 0:4], pPT[:, 256:260]),
                      waits=["t_wscb"], mark="v_swcols")

            if it >= q_start and it > fin_iter - 1:
                for k in range(quant_per_iter):
                    ci = (it - q_start) * quant_per_iter + k
                    if ci >= len(wpieces):
                        continue
                    src, dstq = wp_src(wpieces[ci])
                    scol = swcols[:, 0:1] if wpieces[ci][0] == "w1" else swcols[:, 2:3]
                    tag = f"wq{ci}"
                    S.add(B1, "Y", lambda eng, dma, src=src: dma(WCH[0][:, :], src),
                          waits=[last_q_mark[0] or "v_swcols"], dmas=1, mark=tag + "ld", dsem="dwl0")
                    def op_qv1(eng, scol=scol):
                        return eng.tensor_scalar(WT0[:, :], WCH[0][:, :], scol,
                                                 None, ALU.mult)
                    S.add(B1, "V", op_qv1, waits=[tag + "ld", "v_swcols"], mark=tag + "v1")
                    S.add(B1, "S", lambda eng: eng.activation(
                        WQ16[:, :], WT0[:, :], AF.Sign, bias=0.0, scale=1.0),
                        waits=[tag + "v1", (last_q_mark[0] or "v_swcols")],
                        mark=tag + "sg")
                    def op_qg(eng):
                        eng.tensor_tensor(WT0[:, :], WT0[:, :], WQ16[:, :], ALU.mult)
                        return eng.tensor_copy(WIC[:, :], WT0[:, :])
                    S.add(B1, "V", op_qg, waits=[tag + "sg"], mark=tag + "g")
                    def op_qv2(eng):
                        eng.tensor_tensor(WT0[:, :], WT0[:, :], WIC[:, :], ALU.subtract)
                        eng.tensor_scalar(WT0[:, :], WT0[:, :], 0.5, None, ALU.is_ge)
                        eng.tensor_tensor(WT0[:, :], WT0[:, :], WIC[:, :], ALU.add)
                        eng.tensor_scalar(WT0[:, :], WT0[:, :], 1.0, None, ALU.min)
                        return eng.tensor_tensor(WQ16[:, :], WQ16[:, :], WT0[:, :],
                                                 ALU.mult)
                    S.add(B1, "V", op_qv2, waits=[tag + "g"], mark=tag + "v2")
                    S.add(B1, "G", lambda eng, dma, dstq=dstq: dma(dstq, WQ16[:, :]),
                          waits=[tag + "v2"], dmas=1, mark=tag + "st", dsem="dstwq")
                    last_q_mark[0] = tag + "st"
        assert last_q_mark[0] == f"wq{len(wpieces) - 1}st", "weight quant incomplete"

        S.ops = [o for o in S.ops if o.block <= max_block]
        import os
        _b1lim = int(os.environ.get("B1_PREFIX", "0"))
        if _b1lim:
            _b1 = [o for o in S.ops if o.block == 1]
            _keep = set(id(o) for o in _b1[:_b1lim])
            S.ops = [o for o in S.ops if o.block != 1 or id(o) in _keep]
        S.plan()
        for bi in (0, 1):
            engs = sorted({o.eng for o in S.ops if o.block == bi})
            if not engs:
                continue
            with nc.Block() as blk:
                regs = {"Y": blk.sync, "T": blk.tensor, "V": blk.vector,
                        "S": blk.scalar, "G": blk.gpsimd}
                for ek in engs:
                    def make(ek=ek, bi=bi):
                        def f(eng):
                            S.emit_stream(ek, eng, sems, bi, junk=junkc)
                        return f
                    regs[ek](make())
        ctxA.close()
        S1_counts = dict(S.counts)
        S = Sched(start=S1_counts)
        for _nm in ("dsts", "dstxq", "dstwq"):
            S.marks["A_" + _nm] = (_nm, S1_counts.get(_nm, 0))

        B2 = 2
        ctxB = contextlib.ExitStack()
        allocB = lambda *a: ctxB.enter_context(nc.sbuf_tensor(*a))
        pallocB = lambda *a: ctxB.enter_context(nc.psum_tensor(*a))
        w1qT = allocB("w1qT", [128, 8, DH4], BF16)
        for name in _B_ONLY:
            cst[name] = allocB(name + "_sb", list(_CONST[name].shape), F32)
        def op_ldshifts(eng, dma):
            for name in sorted(_B_ONLY):
                dma(cst[name][:, :], dc[name][:, :])
        S.add(B2, "Y", op_ldshifts, dmas=3, mark="shld", dsem="dshl")
        cw0B = allocB("cw0B", [128, DH4], F32)
        cw1B = allocB("cw1B", [128, DH4], F32)
        cw2B = allocB("cw2B", [128, DH4], F32)
        def op_cwb(eng, dma):
            for si, dst in enumerate((cw0B, cw1B, cw2B)):
                row = d_cwT[si:si + 1, :]
                ap = bass.AP(row.tensor, row.offset,
                             [[0, 128]] + [list(p) for p in row.ap[1:]])
                dma(dst[:, :], ap)
            dma(cbrow[:, :], d_cb[:, :])
        S.add(B2, "G", op_cwb, dmas=4, mark="cwb", dsem="dcwb")
        def op_w1t(eng, dma):
            for dch in range(8):
                dma(w1qT[:, dch, :], d_w1q_scr[:, dch * 128:(dch + 1) * 128], transpose=True)
        S.add(B2, "Y", op_w1t, waits=["A_dstwq"], dmas=8, mark="w1t", dsem="dw1t")

        B3 = 3
        h0 = allocB("h0", [128, DH4], F32)
        h1 = allocB("h1", [128, DH4], F32)
        h2 = allocB("h2", [128, DH4], F32)
        halos = allocB("halos", [4, DH4], F32)
        cbrow = halos[3:4, :]
        ySf = allocB("ySf", [128, DH4], F32)
        y2s16 = allocB("y2s16", [128, 1024], BF16)
        Y2SA = allocB("Y2SA", [128, 8, 128], BF16)
        XQAD = [allocB(f"XQA{i}", [128, 8, 128], BF16) for i in range(2)]
        s1c = allocB("s1c", [128, 4], F32)
        QIC = allocB("QIC", [128, 512], I32)
        QT = allocB("QT", [128, 512], F32)
        qcol = allocB("qcol", [128, 8], F32)
        pBH = pallocB("pBH", [128, 2048], F32)
        pBT = pallocB("pBT", [128, 1024], BF16)

        ldhist = []
        for s_ in range(NSEQ):
            for t in range(n_tiles + 1):
                jj = s_ * (n_tiles + 1) + t
                P = f"a{jj}_"
                Pp = f"a{jj - 1}_"
                do_f = t < n_tiles

                if t == 0:
                    S.add(B3, "V", lambda eng: eng.memset(halos[0:3, :], 0.0),
                          waits=[Pp + "conv3"] if jj else [], mark=P + "hz")

                if do_f:
                    jp = len(ldhist) % 2
                    def op_ldxq(eng, dma, t=t, s_=s_, jp=jp):
                        dma(XQAD[jp][:, :, :], d_xq_scr[s_, t, :, :, :])
                        dma(s1c[:, 2 * jp:2 * jp + 1], d_s1_scr[s_, t, :, :])
                    wq_ = ["w1t", "A_dstxq"]
                    if len(ldhist) >= 2:
                        pf = ldhist[-2]
                        wq_ += [f"a{pf}_ffn1_3"]
                    S.add(B3, "Y", op_ldxq, waits=wq_, dmas=2, mark=P + "dxq",
                          dsem=f"dbxq{jp}")
                    ws1 = [P + "dxq"]
                    if len(ldhist) >= 2:
                        ws1.append(f"a{ldhist[-2]}_hb3")
                    S.add(B3, "V", lambda eng, jp=jp: eng.scalar_tensor_tensor(
                        s1c[:, 2 * jp + 1:2 * jp + 2], s1c[:, 2 * jp:2 * jp + 1],
                        1.0, swcols[:, 1:2],
                        ALU.mult, ALU.mult), waits=ws1, mark=P + "s1tot")
                    ldhist.append(jj)
                    for q in range(4):
                        qb = q % 2
                        wf = [P + "dxq"]
                        if q >= 2:
                            wf.append(P + f"hev{q - 2}")
                        elif jj:
                            wf.append(Pp + f"silu{q + 2}")
                        def op_ffn1(eng, q=q, qb=qb, jp=jp):
                            last = None
                            for k in range(8):
                                for nb in range(2):
                                    last = eng.matmul(
                                        pBH[:, qb * 1024 + nb * 512:qb * 1024 + (nb + 1) * 512],
                                        XQAD[jp][:, k, :],
                                        w1qT[:, k, q * 1024 + nb * 512:q * 1024 + (nb + 1) * 512],
                                        start=(k == 0), stop=(k == 7))
                            return last
                        S.add(B3, "T", op_ffn1, waits=wf, mark=P + f"ffn1_{q}")
                        whb = [P + f"ffn1_{q}", P + "s1tot"]
                        if jj:
                            whb.append(Pp + f"silu{q}")
                        S.add(B3, "S", lambda eng, q=q, qb=qb, jp=jp: eng.activation(
                            h2[:, q * 1024:(q + 1) * 1024],
                            pBH[:, qb * 1024:(qb + 1) * 1024],
                            AF.Identity, bias=0.0,
                            scale=s1c[:, 2 * jp + 1:2 * jp + 2]),
                            waits=whb, mark=P + f"hb{q}")
                        def op_hev(eng, q=q):
                            sl = slice(q * 1024, (q + 1) * 1024)
                            eng.tensor_tensor(h0[:, sl], h2[:, sl], cw0B[:, sl],
                                              ALU.mult)
                            eng.tensor_tensor(h1[:, sl], h2[:, sl], cw1B[:, sl],
                                              ALU.mult)
                            return eng.tensor_tensor(h2[:, sl], h2[:, sl],
                                                     cw2B[:, sl], ALU.mult)
                        whev = [P + f"hb{q}"]
                        if jj == 0 and q == 0:
                            whev.append("cwb")
                        if jj and q == 0:
                            whev.append(Pp + "conv3")
                            if t:
                                whev.append(Pp + "stg")
                        S.add(B3, "V", op_hev, waits=whev, mark=P + f"hev{q}")

                for q in range(4):
                    qb = q % 2
                    wcv = ["cwb", "shld"] if jj == 0 and q == 0 else []
                    if t == 0:
                        wcv.append(P + "hz")
                    if do_f:
                        if q < 2:
                            wcv.append(P + f"hev{q + 2}")
                            wcv.append(P + "hev3")
                        else:
                            wcv.append(P + f"silu{q - 2}")
                            wcv.append(P + "hev3")
                    else:
                        if jj:
                            wcv.append(Pp + "stg")
                        if q >= 2:
                            wcv.append(P + f"silu{q - 2}")
                        elif jj:
                            wcv.append(Pp + f"cadd{q + 2}")
                    if t and q == 0:
                        wcv.append(Pp + "stg")
                    def op_conv(eng, q=q, qb=qb):
                        last = None
                        for nb in range(2):
                            ps = pBH[:, qb * 1024 + nb * 512:qb * 1024 + (nb + 1) * 512]
                            o0 = q * 1024 + nb * 512
                            eng.matmul(ps, cst["sh2_r"][:, :], h0[:, o0:o0 + 512],
                                       start=True, stop=False)
                            eng.matmul(ps, cst["sh1_r"][:, :], h1[:, o0:o0 + 512],
                                       start=False, stop=False)
                            last = eng.matmul(ps, cst["erows_h"][0:4, :],
                                              halos[0:4, o0:o0 + 512],
                                              start=False, stop=True)
                        return last
                    S.add(B3, "T", op_conv, waits=[w for w in wcv if w],
                          mark=P + f"conv{q}")
                    if do_f:
                        S.add(B3, "V", lambda eng, q=q, qb=qb: eng.tensor_tensor(
                            h2[:, q * 1024:(q + 1) * 1024],
                            h2[:, q * 1024:(q + 1) * 1024],
                            pBH[:, qb * 1024:(qb + 1) * 1024], ALU.add),
                            waits=[P + f"conv{q}", P + f"hev{q}"],
                            mark=P + f"cadd{q}")
                        S.add(B3, "S", lambda eng, q=q: eng.activation(
                            ySf[:, q * 1024:(q + 1) * 1024],
                            h2[:, q * 1024:(q + 1) * 1024],
                            AF.Silu, bias=0.0, scale=1.0),
                            waits=[P + f"cadd{q}"] +
                                  ([Pp + f"qb{2 * q + 1}"] if jj else []),
                            mark=P + f"silu{q}")
                    else:
                        S.add(B3, "S", lambda eng, q=q, qb=qb: eng.activation(
                            ySf[:, q * 1024:(q + 1) * 1024],
                            pBH[:, qb * 1024:(qb + 1) * 1024],
                            AF.Silu, bias=0.0, scale=1.0),
                            waits=[P + f"conv{q}"] +
                                  ([Pp + f"qb{2 * q + 1}"] if jj else []),
                            mark=P + f"silu{q}")
                    S.add(B3, "V", lambda eng, q=q: eng.reduce_max(
                        qcol[:, q:q + 1], ySf[:, q * 1024:(q + 1) * 1024], AX.X,
                        apply_absolute_value=True),
                        waits=[P + f"silu{q}"] + ([Pp + "qm2"] if jj and q == 0 else []),
                        mark=P + f"qmx{q}")

                if do_f:
                    def op_stg(eng, dma):
                        dma(halos[0:1, 0:DH4], h0[126:127, :])
                        dma(halos[1:2, 0:DH4], h1[127:128, :])
                        dma(halos[2:3, 0:DH4], h0[127:128, :])
                    S.add(B3, "G", op_stg, waits=[P + "conv3", P + "hev3"],
                          dmas=3, mark=P + "stg", dsem="dhalo")

                def op_qm2(eng):
                    eng.tensor_tensor(qcol[:, 0:1], qcol[:, 0:1], qcol[:, 1:2], ALU.max)
                    eng.tensor_tensor(qcol[:, 2:3], qcol[:, 2:3], qcol[:, 3:4], ALU.max)
                    eng.tensor_tensor(qcol[:, 0:1], qcol[:, 0:1], qcol[:, 2:3], ALU.max)
                    eng.tensor_scalar(qcol[:, 1:2], qcol[:, 0:1], 1e-5, None, ALU.max)
                    eng.reciprocal(qcol[:, 2:3], qcol[:, 1:2])
                    eng.tensor_tensor(qcol[:, 6:7], qcol[:, 1:2], qcol[:, 2:3], ALU.mult)
                    eng.tensor_scalar(qcol[:, 6:7], qcol[:, 6:7], -1.0, 2.0,
                                      ALU.mult, ALU.add)
                    eng.tensor_tensor(qcol[:, 2:3], qcol[:, 2:3], qcol[:, 6:7], ALU.mult)
                    eng.tensor_scalar(qcol[:, 3:4], qcol[:, 2:3], 127.0, None, ALU.mult)
                    eng.tensor_scalar(qcol[:, 4:5], qcol[:, 1:2], 1.0 / 127.0, None, ALU.mult)
                    return eng.scalar_tensor_tensor(qcol[:, 5:6], qcol[:, 4:5], 1.0,
                                                    swcols[:, 3:4], ALU.mult, ALU.mult)
                S.add(B3, "V", op_qm2, waits=[P + "qmx0", P + "qmx1", P + "qmx2",
                                              P + "qmx3"], mark=P + "qm2")
                for qq in range(4):
                    for hh in range(2):
                        seg = qq * 2 + hh
                        sl = slice(seg * 512, (seg + 1) * 512)
                        osl = slice(hh * 512, (hh + 1) * 512)
                        wsg = [P + f"silu{seg // 2}"]
                        if hh == 0:
                            if qq:
                                wsg.append(P + f"y2t{qq - 1}")
                            elif jj:
                                wsg.append(Pp + "y2t3")
                        S.add(B3, "S", lambda eng, sl=sl, osl=osl: eng.activation(
                            y2s16[:, osl], ySf[:, sl], AF.Sign, bias=0.0, scale=1.0),
                            waits=wsg, mark=P + f"qs{seg}")
                        wqb = [P + "qm2", P + f"qs{seg}"]
                        if seg:
                            wqb.append(P + f"qd{seg - 1}")
                        elif jj:
                            wqb.append(Pp + "qd7")
                        S.add(B3, "V", lambda eng, sl=sl, osl=osl: (
                            eng.scalar_tensor_tensor(QT[:, :], ySf[:, sl],
                                                     qcol[:, 3:4], y2s16[:, osl],
                                                     ALU.mult, ALU.mult),
                            eng.tensor_copy(QIC[:, :], QT[:, :]))[-1],
                            waits=wqb, mark=P + f"qb{seg}")
                        S.add(B3, "V", lambda eng, sl=sl, osl=osl: (
                            eng.tensor_tensor(QT[:, :], QT[:, :], QIC[:, :],
                                              ALU.subtract),
                            eng.scalar_tensor_tensor(QT[:, :], QT[:, :], 0.5,
                                                     QIC[:, :], ALU.is_ge, ALU.add),
                            eng.scalar_tensor_tensor(y2s16[:, osl], QT[:, :], 127.0,
                                                     y2s16[:, osl], ALU.min,
                                                     ALU.mult))[-1],
                            waits=[P + f"qb{seg}"], mark=P + f"qd{seg}")
                    def op_y2t(eng):
                        last = None
                        for k in range(8):
                            last = eng.transpose(pBT[:, k * 128:(k + 1) * 128],
                                                 y2s16[:, k * 128:(k + 1) * 128],
                                                 cst["idn16"][:, :])
                        return last
                    wy = [P + f"qd{2 * qq + 1}"]
                    if qq:
                        wy.append(P + f"y2a{qq - 1}")
                    elif jj:
                        wy.append(Pp + "y2a3")
                    S.add(B3, "T", op_y2t, waits=wy, mark=P + f"y2t{qq}")
                    wy2a = [P + f"y2t{qq}",
                            (P + f"y2st{qq - 1}") if qq else
                            ((Pp + "y2st3") if jj else None)]
                    S.add(B3, "S", lambda eng: eng.copy(Y2SA[:, :, :], pBT[:, 0:1024]),
                          waits=[w for w in wy2a if w], mark=P + f"y2a{qq}")
                    def op_y2st(eng, dma, qq=qq, t=t, s_=s_):
                        dma(d_y2a_scr[s_, t, :, qq * 8:(qq + 1) * 8, :], Y2SA[:, :, :])
                        if qq == 0:
                            dma(d_s2_scr[s_, t, :, :], qcol[:, 5:6])
                    S.add(B3, "G", op_y2st,
                          waits=[P + f"y2a{qq}"], dmas=2 if qq == 0 else 1,
                          mark=P + f"y2st{qq}", dsem="dy2a")

        S.ops = [o for o in S.ops if o.block <= max_block]
        _b3lim = int(os.environ.get("B3_PREFIX", "0"))
        if _b3lim:
            _b3 = [o for o in S.ops if o.block == 3]
            _keep = set(id(o) for o in _b3[:_b3lim])
            S.ops = [o for o in S.ops if o.block != 3 or id(o) in _keep]
        S.plan()
        for bi in (2, 3):
            engs = sorted({o.eng for o in S.ops if o.block == bi})
            if engs:
                with nc.Block() as blk:
                    regs = {"Y": blk.sync, "T": blk.tensor, "V": blk.vector,
                            "S": blk.scalar, "G": blk.gpsimd}
                    for ek in engs:
                        def make(ek=ek, bi=bi):
                            def f(eng):
                                S.emit_stream(ek, eng, sems, bi, junk=junkc)
                            return f
                        regs[ek](make())
        ctxB.close()
        B3_counts = dict(S.counts)
        S = Sched(start=B3_counts)
        for _nm in ("dsts", "dstwq", "dy2a"):
            S.marks["B_" + _nm] = (_nm, B3_counts.get(_nm, 0))

        B4 = 4
        w2qT = alloc("w2qT", [128, 32, 1024], BF16)
        s2c = alloc("s2c", [128, 2], F32)
        Y2L = [alloc(f"Y2L{i}", [128, 32, 128], BF16) for i in range(2)]
        xBb = [alloc(f"xBb{i}", [128, D], F32) for i in range(2)]
        sBb = [alloc(f"sBb{i}", [128, D], BF16) for i in range(2)]
        OUTS = [alloc(f"OUTS{i}", [128, D], F32) for i in range(2)]
        pBG = palloc("pBG", [128, 1024], F32)

        for wg_ in range(4):
            def op_w2t(eng, dma, wg_=wg_):
                for k in range(wg_ * 8, (wg_ + 1) * 8):
                    dma(w2qT[:, k, :], d_w2q_scr[:, k * 128:(k + 1) * 128],
                        transpose=True)
            S.add(B4, "Y", op_w2t, waits=["B_dstwq"] if wg_ == 0 else [],
                  dmas=8, mark=f"w2t{wg_}", dsem="dw2t")
        S.add(B4, "V", lambda eng: (
            eng.memset(xBb[0][:, :], 0.0), eng.memset(xBb[1][:, :], 0.0),
            eng.memset(sBb[0][:, :], 0.0), eng.memset(sBb[1][:, :], 0.0))[-1],
            mark="b4z")

        for s_ in range(NSEQ):
            for t in range(n_tiles + 1):
                jj = s_ * (n_tiles + 1) + t
                P = f"c{jj}_"
                Pp = f"c{jj - 1}_"
                Ppp = f"c{jj - 2}_"
                jb = jj % 2
                r0 = t * T - 1
                lo = 1 if t == 0 else 0
                hi = 1 if t == n_tiles else 128
                def op_ld(eng, dma, t=t, s_=s_, jb=jb, r0=r0, lo=lo, hi=hi):
                    dma(Y2L[jb][:, :, :], d_y2a_scr[s_, t, :, :, :])
                    dma(s2c[:, jb:jb + 1], d_s2_scr[s_, t, :, :])
                    dma(xBb[jb][lo:hi, :], d_x[s_, r0 + lo:r0 + hi, :])
                    dma(sBb[jb][lo:hi, :], d_s_scr[s_, r0 + lo:r0 + hi, :])
                wl = ["B_dy2a", "B_dsts", "b4z"]
                if jj >= 2:
                    wl += [Ppp + "ffn2_31", Ppp + "comb"]
                S.add(B4, "Y", op_ld, waits=wl, dmas=4, mark=P + "ld",
                      dsem=f"dbl{jb}")
                for k in range(32):
                    def op_ffn2(eng, k=k, jb=jb):
                        last = None
                        for nb in range(2):
                            last = eng.matmul(pBG[:, nb * 512:(nb + 1) * 512],
                                              Y2L[jb][:, k, :],
                                              w2qT[:, k, nb * 512:(nb + 1) * 512],
                                              start=(k == 0), stop=(k == 31))
                        return last
                    wf = [P + "ld"]
                    if jj == 0 and k % 8 == 0:
                        wf.append(f"w2t{k // 8}")
                    if k == 0 and jj:
                        wf.append(Pp + "gsilu")
                    S.add(B4, "T", op_ffn2, waits=wf,
                          mark=(P + f"ffn2_{k}") if k in (0, 31) else None)
                wgs = [P + "ffn2_31"]
                if jj >= 2:
                    wgs.append(Ppp + "outst")
                S.add(B4, "S", lambda eng, jb=jb: eng.activation(
                    OUTS[jb][:, :], pBG[:, :], AF.Silu, bias=0.0,
                    scale=s2c[:, jb:jb + 1]),
                    waits=wgs, mark=P + "gsilu")
                def op_comb(eng, jb=jb):
                    eng.tensor_tensor(OUTS[jb][:, :], OUTS[jb][:, :], sBb[jb][:, :],
                                      ALU.mult)
                    return eng.tensor_tensor(OUTS[jb][:, :], OUTS[jb][:, :],
                                             xBb[jb][:, :], ALU.add)
                S.add(B4, "V", op_comb, waits=[P + "gsilu", P + "ld"], mark=P + "comb")
                S.add(B4, "G", lambda eng, dma, t=t, s_=s_, jb=jb, r0=r0, lo=lo, hi=hi:
                      dma(d_out[s_, r0 + lo:r0 + hi, :], OUTS[jb][lo:hi, :]),
                      waits=[P + "comb"], dmas=1, mark=P + "outst",
                      dsem=f"dout{jb}")

        S.ops = [o for o in S.ops if o.block <= max_block]
        S.plan()
        for bi in (4,):
            engs = sorted({o.eng for o in S.ops if o.block == bi})
            if not engs:
                continue
            with nc.Block() as blk:
                regs = {"Y": blk.sync, "T": blk.tensor, "V": blk.vector,
                        "S": blk.scalar, "G": blk.gpsimd}
                for ek in engs:
                    def make(ek=ek, bi=bi):
                        def f(eng):
                            S.emit_stream(ek, eng, sems, bi, junk=junkc)
                        return f
                    regs[ek](make())
        with nc.Block() as blk:
            @blk.gpsimd
            def _(g):
                for k, v in sorted(S.counts.items()):
                    if k in sems and v and k not in ("T", "V", "S", "G"):
                        g.wait_ge(sems[k], v)
    return nc


def kernel(L0_attn_x, L0_attn_Wq, L0_attn_Wk, L0_attn_Wv, L0_attn_Wg, L0_attn_bg,
           L0_attn_Wo, L0_ffn_w1, L0_ffn_conv_w, L0_ffn_conv_b, L0_ffn_w2,
           _debug=False):
    x = np.ascontiguousarray(np.asarray(L0_attn_x, np.float32))
    nc = build_program(debug=_debug)
    base = {
        "wq": np.ascontiguousarray(L0_attn_Wq, np.float32),
        "wk": np.ascontiguousarray(L0_attn_Wk, np.float32),
        "wv": np.ascontiguousarray(L0_attn_Wv, np.float32),
        "wg": np.ascontiguousarray(L0_attn_Wg, np.float32),
        "bg": np.ascontiguousarray(np.asarray(L0_attn_bg, np.float32).reshape(1, D)),
        "wo": np.ascontiguousarray(L0_attn_Wo, np.float32),
        "w1": np.ascontiguousarray(L0_ffn_w1, np.float32),
        "w2": np.ascontiguousarray(L0_ffn_w2, np.float32),
        "cwT": np.ascontiguousarray(np.asarray(L0_ffn_conv_w, np.float32)[:, 0, :].T),
        "cb": np.ascontiguousarray(np.asarray(L0_ffn_conv_b, np.float32).reshape(1, DH4)),
    }
    for name, arr in _CONST.items():
        base[name] = arr
    in_maps = []
    for c in range(8):
        m = dict(base)
        m["x"] = np.ascontiguousarray(x[2 * c:2 * c + 2])
        in_maps.append(m)
    res = run_bass_kernel_spmd(nc, in_maps, core_ids=list(range(8)))
    out = np.concatenate([np.asarray(r["out"], np.float32) for r in res.results], axis=0)
    if _debug:
        kernel._dbg = res.results
    return out.reshape(16, SEQLEN, D)


